# version 09
# baseline (speedup 1.0000x reference)
import numpy as np
import concourse.bass as bass
import concourse.mybir as mybir
from concourse.bass_utils import run_bass_kernel_spmd

F32 = mybir.dt.float32
BF16 = mybir.dt.bfloat16
AF = mybir.ActivationFunctionType
ALU = mybir.AluOpType
AX = mybir.AxisListType.X

D = 4096
DH = 1024
NDC = 32
NHC = 8
ER = 128
SMW = 137
EPS = 1e-5
SCALE = 1.0 / 32.0


class Ctr:
    def __init__(self, nc, name):
        self.guard = nc.semaphore(name)
        self.sem = self.guard.__enter__()
        self.val = 0

    def inc(self, ins, n=1):
        ins.then_inc(self.sem, n)
        self.val += n
        return ins


class Alloc:
    def __init__(self, nc):
        self.nc = nc
        self.live = {}
        self.order = {"left": [], "right": []}

    def sbuf(self, name, shape, dt, side="right"):
        g = self.nc.sbuf_tensor(name, shape, dt, side=side)
        t = g.__enter__()
        self.live[name] = (g, side)
        self.order[side].append(name)
        return t

    def free(self, *names):
        for n in names:
            g, side = self.live.pop(n)
            assert self.order[side][-1] == n, f"non-LIFO free {n} (top={self.order[side][-1]})"
            self.order[side].pop()
            g.__exit__(None, None, None)

    def free_all(self):
        for side in ("right", "left"):
            for n in reversed(list(self.order[side])):
                g, _ = self.live.pop(n)
                g.__exit__(None, None, None)
            self.order[side] = []



def build_program(n_all, n_own):
    nt_all = n_all // 128
    nt_own = n_own // 128
    ng_all = n_all // 512
    ng_own = n_own // 512
    nks = n_all // 512

    nc = bass.Bass("TRN2", target_bir_lowering=False, debug=False)

    XB = nc.dram_tensor("XB", [n_all, D], F32, kind="ExternalInput")
    WQKV = nc.dram_tensor("WQKV", [D, 3 * DH], F32, kind="ExternalInput")
    BQKV = nc.dram_tensor("BQKV", [1, 3 * DH], F32, kind="ExternalInput")
    WO = nc.dram_tensor("WO", [DH, D], F32, kind="ExternalInput")
    BO = nc.dram_tensor("BO", [1, D], F32, kind="ExternalInput")
    WBT = nc.dram_tensor("WBT", [D, D], F32, kind="ExternalInput")
    BBASE = nc.dram_tensor("BBASE", [1, D], F32, kind="ExternalInput")
    SMALLW = nc.dram_tensor("SMALLW", [D, SMW], F32, kind="ExternalInput")
    BTHR = nc.dram_tensor("BTHR", [1, 1], F32, kind="ExternalInput")
    BW = nc.dram_tensor("BW", [ER, D], F32, kind="ExternalInput")
    OUT = nc.dram_tensor("OUT", [n_own, D], F32, kind="ExternalOutput")

    XLNT = nc.dram_tensor("XLNT", [ng_all, NDC, 128, 512], BF16, kind="Internal")
    QTS = nc.dram_tensor("QTS", [ng_own, NHC, 128, 512], BF16, kind="Internal")
    X1S = nc.dram_tensor("X1S", [n_own, D], F32, kind="Internal")
    ZTS = nc.dram_tensor("ZTS", [nt_own, NDC, 128, 128], BF16, kind="Internal")

    al = Alloc(nc)
    psg = nc.psum_tensor("ps", [128, 4096], F32)
    ps = psg.__enter__()

    c_gw = Ctr(nc, "c_gw")
    cx = [Ctr(nc, f"cx{i}") for i in range(2)]
    cqs = [Ctr(nc, f"cqs{i}") for i in range(2)]
    csw = [Ctr(nc, f"csw{i}") for i in range(2)]
    cgx = [Ctr(nc, f"cgx{i}") for i in range(2)]
    csz = [Ctr(nc, f"csz{i}") for i in range(2)]
    cso = [Ctr(nc, f"cso{i}") for i in range(2)]
    ctz = [Ctr(nc, f"ctz{i}") for i in range(2)]
    c_tp = Ctr(nc, "c_tp")
    c_tq = Ctr(nc, "c_tq")
    c_ta = Ctr(nc, "c_ta")
    cwb = [Ctr(nc, f"cwb{i}") for i in range(2)]
    c_pe = Ctr(nc, "c_pe")
    c_v = Ctr(nc, "c_v")
    c_a = Ctr(nc, "c_a")
    ctrs = [c_gw, cx[0], cx[1], cqs[0], cqs[1], csw[0], csw[1], cgx[0], cgx[1], csz[0], csz[1],
            cso[0], cso[1], ctz[0], ctz[1], c_tp, c_tq, c_ta, cwb[0], cwb[1],
            c_pe, c_v, c_a]

    def snap():
        return {id(c): c.val for c in ctrs}

    ones = al.sbuf("ones", [1, 512], BF16, side="left")
    bqkv_sb = al.sbuf("bqkv_sb", [1, 3 * DH], BF16, side="left")
    bthr_sb = al.sbuf("bthr_sb", [1, 1], BF16, side="left")
    wq = al.sbuf("wq", [128, NDC, DH], BF16, side="left")

    blkg = nc.Block()
    blk = blkg.__enter__()
    if True:
        @blk.gpsimd
        def _(g):
            g.memset(ones[:, :], 1.0)
            c_gw.inc(g.dma_start(bqkv_sb[:, :], BQKV[:, :]), 16)
            c_gw.inc(g.dma_start(bthr_sb[:, :], BTHR[:, :]), 16)
            for dck in range(NDC):
                c_gw.inc(g.dma_start(wq[:, dck, :], WQKV[dck * 128:(dck + 1) * 128, 0:DH]), 16)
    wq_loaded = c_gw.val

    qtb = al.sbuf("qtb", [128, NHC, 512], BF16)
    qtb_1 = al.sbuf("qtb_1", [128, NHC, 512], BF16)
    xbuf = [al.sbuf(f"xbuf{i}", [128, D], F32) for i in range(2)]
    zbuf = [al.sbuf(f"zbuf{i}", [128, D], BF16) for i in range(2)]
    gbuf = [al.sbuf(f"gbuf{i}", [128, NDC, 512], BF16) for i in range(2)]
    stats = [al.sbuf(f"stats{i}", [128, 8, 6], F32) for i in range(2)]
    mv2 = [al.sbuf(f"mv2_{i}", [128, 2], F32) for i in range(2)]
    rstd = [al.sbuf(f"rstd{i}", [128, 1], F32) for i in range(2)]
    rstd2 = [al.sbuf(f"rstd2{i}", [128, 1], F32) for i in range(2)]
    nbias = [al.sbuf(f"nbias{i}", [128, 1], F32) for i in range(2)]
    tmp1 = [al.sbuf(f"tmp1{i}", [128, 1], F32) for i in range(2)]
    y2 = [al.sbuf(f"y2_{i}", [128, 1], F32) for i in range(2)]

    B = snap()
    bx = [B[id(cx[0])], B[id(cx[1])]]
    bv, ba, bpe = B[id(c_v)], B[id(c_a)], B[id(c_pe)]
    bqs = [B[id(cqs[0])], B[id(cqs[1])]]
    btz = [B[id(ctz[0])], B[id(ctz[1])]]
    bsw = [B[id(cgx[0])], B[id(cgx[1])]]
    xlnt_written = [(None, 0)] * ng_all
    NT = nt_all

    def v_bn(t):
        return bv + 7 * t + 1

    def v_aggr(t):
        return bv + 7 * t + 2

    def v_eps(t):
        return bv + 7 * t + 3

    def v_y2(j):
        return bv + 7 * j + 4

    def v_nbias(j):
        return bv + 7 * j + 7

    def a_ln(t):
        return ba + 3 * t + 1

    def a_exp(t):
        return ba + 3 * t + 2

    def a_id(j):
        return ba + 3 * j + 3

    def v_tz(j):
        return btz[j % 2] + (j // 2 + 1) * 16

    qtb2 = [qtb, qtb_1]
    ev_at = {}
    for g in range(ng_own):
        j = 4 * g + 8
        ev_at.setdefault(j if j < NT else NT, []).append(g)

    V = {}
    _c = bv
    for t in range(NT + 1):
        if t >= 1:
            for k in ("y2", "t2", "rstd2", "nbias"):
                _c += 1
                V[(k, t - 1)] = _c
        if t < NT:
            for k in ("bn", "aggr", "eps"):
                _c += 1
                V[(k, t)] = _c
        for g in ev_at.get(t, []):
            for f in range(8):
                _c += 1
                V[("ev", g, f)] = _c
    p01_v_total = _c

    if True:
        @blk.sync
        def _(sy):
            def qts_store(g):
                sy.wait_ge(c_v.sem, V[("ev", g, 7)])
                c_ss.inc(sy.dma_start(QTS[g, :, :, :].rearrange("f p t -> p f t"),
                                      qtb2[g % 2][:, :, :]), 16)
            def sp_tr(j):
                sj = j % 2
                g, tl = j // 4, j % 4
                gs = g % 2
                if j >= 2:
                    sy.wait_ge(ctz[sj].sem, v_tz(j - 2))
                if tl == 0 and g >= 2:
                    sy.wait_ge(cgx[gs].sem, bsw[gs] + (g // 2) * 16)
                    if g - 2 < ng_own:
                        sy.wait_ge(c_pe.sem, bpe + (g - 1) * 8)
                sy.wait_ge(c_a.sem, a_id(j))
                ctz[sj].inc(sy.dma_start(gbuf[gs][:, :, tl * 128:(tl + 1) * 128],
                                         zbuf[sj][:, :].rearrange("p (c f) -> p c f", c=NDC),
                                         transpose=True), 16)

            for t in range(nt_all):
                s = t % 2
                if t >= 2:
                    sy.wait_ge(c_a.sem, a_id(t - 2))
                    sy.wait_ge(c_v.sem, V[("bn", t - 2)])
                cx[s].inc(sy.dma_start(xbuf[s][:, :], XB[t * 128:(t + 1) * 128, :]), 16)
                if t >= 1:
                    sp_tr(t - 1)
            sp_tr(nt_all - 1)


        @blk.vector
        def _(v):
            def newton(j):
                sj = j % 2
                v.wait_ge(c_a.sem, a_exp(j))
                c_v.inc(v.tensor_mul(y2[sj][:, :], rstd[sj][:, :], rstd[sj][:, :]))
                v.wait_ge(c_v.sem, V[("y2", j)])
                c_v.inc(v.tensor_scalar(y2[sj][:, :], y2[sj][:, :], tmp1[sj][:, 0:1], -0.5,
                                        ALU.mult, ALU.mult))
                v.wait_ge(c_v.sem, V[("t2", j)])
                if j >= 2:
                    v.wait_ge(c_a.sem, a_id(j - 2))
                c_v.inc(v.scalar_tensor_tensor(rstd2[sj][:, :], y2[sj][:, :], 1.5,
                                               rstd[sj][:, :], ALU.add, ALU.mult))
                v.wait_ge(c_v.sem, V[("rstd2", j)])
                c_v.inc(v.tensor_scalar(nbias[sj][:, :], mv2[sj][:, 0:1], rstd2[sj][:, 0:1],
                                        -1.0, ALU.mult, ALU.mult))

            def q_evict(g):
                if g >= 2:
                    v.wait_ge(cqs[g % 2].sem, bqs[g % 2] + (g // 2) * 16)
                for f in range(8):
                    v.wait_ge(c_pe.sem, bpe + g * 8 + f + 1)
                    c_v.inc(v.tensor_scalar_mul(qtb2[g % 2][:, f, :],
                                                ps[:, f * 512:(f + 1) * 512], SCALE))

            for t in range(nt_all):
                s = t % 2
                if t >= 1:
                    newton(t - 1)
                v.wait_ge(cx[s].sem, bx[s] + (t // 2 + 1) * 16)
                for i in range(8):
                    ins = v.bn_stats(stats[s][:, i, :], xbuf[s][:, i * 512:(i + 1) * 512])
                c_v.inc(ins)
                v.wait_ge(c_v.sem, V[("bn", t)])
                c_v.inc(v.bn_aggr(mv2[s][:, :], stats[s][:, :, :].rearrange("p c f -> p (c f)")))
                v.wait_ge(c_v.sem, V[("aggr", t)])
                if t >= 2:
                    v.wait_ge(c_a.sem, a_ln(t - 2))
                c_v.inc(v.tensor_scalar_add(tmp1[s][:, :], mv2[s][:, 1:2], EPS))
                for g in ev_at.get(t, []):
                    q_evict(g)
            newton(nt_all - 1)
            for g in ev_at.get(NT, []):
                q_evict(g)

        @blk.scalar
        def _(sc):
            def ident(j):
                sj = j % 2
                sc.wait_ge(c_v.sem, V[("nbias", j)])
                if j >= 2:
                    sc.wait_ge(ctz[sj].sem, v_tz(j - 2))
                c_a.inc(sc.activation(zbuf[sj][:, :], xbuf[sj][:, :], AF.Identity,
                                      bias=nbias[sj][:, :], scale=rstd2[sj][:, :]))

            def xlnt_write(g):
                gs = g % 2
                sc.wait_ge(ctz[0].sem, btz[0] + (2 * g + 2) * 16)
                sc.wait_ge(ctz[1].sem, btz[1] + (2 * g + 2) * 16)
                cgx[gs].inc(sc.dma_start(XLNT[g, :, :, :].rearrange("c p t -> p c t"),
                                         gbuf[gs][:, :, :]), 16)
                xlnt_written[g] = (cgx[gs].sem, cgx[gs].val)

            def qts_store(g):
                sc.wait_ge(c_v.sem, V[("ev", g, 7)])
                cqs[g % 2].inc(sc.dma_start(QTS[g, :, :, :].rearrange("f p t -> p f t"),
                                            qtb2[g % 2][:, :, :]), 16)

            st_due = {}
            for g in range(ng_own):
                st_due.setdefault(min(4 * g + 10, NT), []).append(g)
            for t in range(nt_all):
                s = t % 2
                if t >= 1:
                    ident(t - 1)
                    if (t - 1) % 4 == 3:
                        xlnt_write((t - 1) // 4)
                for g in st_due.get(t, []):
                    qts_store(g)
                sc.wait_ge(c_v.sem, V[("eps", t)])
                c_a.inc(sc.activation(rstd[s][:, :], tmp1[s][:, :], AF.Ln))
                sc.wait_ge(c_a.sem, a_ln(t))
                c_a.inc(sc.activation(rstd[s][:, :], rstd[s][:, :], AF.Exp, scale=-0.5))
            ident(nt_all - 1)
            xlnt_write(ng_all - 1)
            for g in st_due.get(NT, []):
                qts_store(g)

    if True:
        @blk.tensor
        def _(t):
            t.wait_ge(c_gw.sem, wq_loaded)
            for g in range(ng_own):
                t.wait_ge(ctz[0].sem, btz[0] + (2 * g + 2) * 16)
                t.wait_ge(ctz[1].sem, btz[1] + (2 * g + 2) * 16)
                if g > 0:
                    t.wait_ge(c_v.sem, V[("ev", g - 1, 7)])
                for dck in range(NDC):
                    for f in range(8):
                        t.matmul(ps[:, f * 512:(f + 1) * 512], wq[:, dck, f * 128:(f + 1) * 128],
                                 gbuf[g % 2][:, dck, :], start=(dck == 0), stop=False)
                for f in range(8):
                    c_pe.inc(t.matmul(ps[:, f * 512:(f + 1) * 512],
                                      bqkv_sb[:, f * 128:(f + 1) * 128],
                                      ones[:, :], start=False, stop=True))

    al.free("y2_1", "y2_0", "tmp11", "tmp10", "nbias1", "nbias0", "rstd21", "rstd20",
            "rstd1", "rstd0", "mv2_1", "mv2_0", "stats1", "stats0",
            "gbuf1", "gbuf0", "zbuf1", "zbuf0", "xbuf1", "xbuf0",
            "qtb_1", "qtb")
    al.free("wq")
    p1_end = snap()

    kt = al.sbuf("kt", [128, NHC, n_all], BF16, side="left")
    xg = [al.sbuf(f"xg{i}", [128, NDC, 512], BF16) for i in range(2)]
    wk = [al.sbuf(f"wk{i}", [128, NDC, 512], BF16) for i in range(2)]

    B = snap()
    bgw = B[id(c_gw)]
    bx = [B[id(cx[0])], B[id(cx[1])]]
    bpe, bv = B[id(c_pe)], B[id(c_v)]
    wk_loaded = [(cwb[0].sem, cwb[0].val + NDC * 16), (cwb[1].sem, cwb[1].val + NDC * 16)]

    blkg.__exit__(None, None, None)
    blkg = nc.Block()
    blk = blkg.__enter__()
    if True:
        @blk.gpsimd
        def _(g):
            g.wait_ge(c_pe.sem, p1_end[id(c_pe)])
            g.wait_ge(c_a.sem, p1_end[id(c_a)])
            g.wait_ge(ctz[0].sem, p1_end[id(ctz[0])])
            g.wait_ge(ctz[1].sem, p1_end[id(ctz[1])])
            g.wait_ge(cgx[0].sem, p1_end[id(cgx[0])])
            g.wait_ge(cgx[1].sem, p1_end[id(cgx[1])])
            g.wait_ge(cqs[0].sem, p1_end[id(cqs[0])])
            g.wait_ge(cqs[1].sem, p1_end[id(cqs[1])])
            for fh in range(2):
                for dck in range(NDC):
                    cwb[fh].inc(g.dma_start(
                        wk[fh][:, dck, :],
                        WQKV[dck * 128:(dck + 1) * 128, DH + fh * 512:DH + (fh + 1) * 512]), 16)

        @blk.sync
        def _(sy):
            sy.wait_ge(cx[0].sem, bx[0])
            sy.wait_ge(cx[1].sem, bx[1])
            sy.wait_ge(cqs[0].sem, p1_end[id(cqs[0])])
            sy.wait_ge(cqs[1].sem, p1_end[id(cqs[1])])
            sy.wait_ge(c_a.sem, p1_end[id(c_a)])
            sy.wait_ge(ctz[0].sem, p1_end[id(ctz[0])])
            sy.wait_ge(ctz[1].sem, p1_end[id(ctz[1])])
            sy.wait_ge(cgx[0].sem, p1_end[id(cgx[0])])
            sy.wait_ge(cgx[1].sem, p1_end[id(cgx[1])])
            for it in range(2 * ng_all):
                g = it % ng_all
                if it >= 2:
                    sy.wait_ge(c_pe.sem, bpe + (it - 1) * 4)
                cx[it % 2].inc(sy.dma_start(xg[it % 2][:, :, :],
                                            XLNT[g, :, :, :].rearrange("c p t -> p c t")), 16)

        @blk.tensor
        def _(t):
            t.wait_ge(c_v.sem, bv)
            for fh in range(2):
                t.wait_ge(*wk_loaded[fh])
                for g in range(ng_all):
                    it = fh * ng_all + g
                    t.wait_ge(cx[it % 2].sem, bx[it % 2] + (it // 2 + 1) * 16)
                    if it > 0:
                        t.wait_ge(c_v.sem, bv + it * 4)
                    for dck in range(NDC):
                        for f in range(4):
                            t.matmul(ps[:, f * 512:(f + 1) * 512],
                                     wk[fh][:, dck, f * 128:(f + 1) * 128],
                                     xg[it % 2][:, dck, :], start=(dck == 0), stop=False)
                    for f in range(4):
                        c_pe.inc(t.matmul(
                            ps[:, f * 512:(f + 1) * 512],
                            bqkv_sb[:, DH + fh * 512 + f * 128:DH + fh * 512 + (f + 1) * 128],
                            ones[:, :], start=False, stop=True))

        @blk.vector
        def _(v):
            for fh in range(2):
                for g in range(ng_all):
                    it = fh * ng_all + g
                    for f in range(4):
                        v.wait_ge(c_pe.sem, bpe + it * 4 + f + 1)
                        c_v.inc(v.tensor_copy(kt[:, fh * 4 + f, g * 512:(g + 1) * 512],
                                              ps[:, f * 512:(f + 1) * 512]))

    al.free("wk1", "wk0", "xg1", "xg0")
    p2_end = snap()

    vsb = al.sbuf("vsb", [128, nt_all, DH], BF16, side="left")
    xv = [al.sbuf(f"xv{i}", [128, NDC, 256], BF16) for i in range(2)]
    wv = al.sbuf("wv", [128, NDC, 512], BF16)

    B = snap()
    bgw = B[id(c_gw)]
    bx = [B[id(cx[0])], B[id(cx[1])]]
    bpe, bv = B[id(c_pe)], B[id(c_v)]
    wv_loaded = [bgw + NDC * 16, bgw + 2 * NDC * 16]
    pe_dhh_done = [bpe + ng_all * 4, bpe + 2 * ng_all * 4]

    blkg.__exit__(None, None, None)
    blkg = nc.Block()
    blk = blkg.__enter__()
    if True:
        @blk.gpsimd
        def _(g):
            g.wait_ge(c_pe.sem, p2_end[id(c_pe)])
            for dhh in range(2):
                if dhh == 1:
                    g.wait_ge(c_pe.sem, pe_dhh_done[0])
                for dck in range(NDC):
                    c_gw.inc(g.dma_start(
                        wv[:, dck, :],
                        WQKV[dck * 128:(dck + 1) * 128, 2 * DH + dhh * 512:2 * DH + (dhh + 1) * 512]), 16)

        @blk.sync
        def _(sy):
            sy.wait_ge(cx[0].sem, bx[0])
            sy.wait_ge(cx[1].sem, bx[1])
            sy.wait_ge(c_pe.sem, p2_end[id(c_pe)])
            for it in range(2 * ng_all):
                g = it % ng_all
                for hg in range(2):
                    if it >= 1:
                        sy.wait_ge(c_pe.sem, bpe + (it - 1) * 4 + hg * 2 + 2)
                    cx[hg].inc(sy.dma_start(
                        xv[hg][:, :, :],
                        XLNT[g, :, :, hg * 256:(hg + 1) * 256].rearrange("c p t -> p c t")), 16)

        @blk.tensor
        def _(t):
            t.wait_ge(c_v.sem, bv)
            for dhh in range(2):
                t.wait_ge(c_gw.sem, wv_loaded[dhh])
                for g in range(ng_all):
                    it = dhh * ng_all + g
                    for tl in range(4):
                        hg = tl // 2
                        bank = tl
                        t.wait_ge(cx[hg].sem, bx[hg] + (it + 1) * 16)
                        if it > 0:
                            t.wait_ge(c_v.sem, bv + (it - 1) * 4 + tl + 1)
                        for dck in range(NDC):
                            t.matmul(ps[:, bank * 512:(bank + 1) * 512],
                                     xv[hg][:, dck, (tl % 2) * 128:(tl % 2 + 1) * 128],
                                     wv[:, dck, :], start=(dck == 0), stop=False)
                        c_pe.inc(t.matmul(
                            ps[:, bank * 512:(bank + 1) * 512], ones[:, 0:128],
                            bqkv_sb[:, 2 * DH + dhh * 512:2 * DH + (dhh + 1) * 512],
                            start=False, stop=True))

        @blk.vector
        def _(v):
            for dhh in range(2):
                for g in range(ng_all):
                    it = dhh * ng_all + g
                    for tl in range(4):
                        v.wait_ge(c_pe.sem, bpe + it * 4 + tl + 1)
                        c_v.inc(v.tensor_copy(
                            vsb[:, g * 4 + tl, dhh * 512:(dhh + 1) * 512],
                            ps[:, tl * 512:(tl + 1) * 512]))

    al.free("wv", "xv1", "xv0")
    p3_end = snap()

    wo0 = al.sbuf("wo0", [128, 2, D], BF16)
    aot = al.sbuf("aot", [128, NHC, n_own], BF16)
    qtt = [al.sbuf(f"qtt{i}", [128, NHC, 128], BF16) for i in range(2)]
    probs = al.sbuf("probs", [128, n_all], BF16)
    probsTA = al.sbuf("probsTA", [128, nt_all - 8, 128], BF16)
    probsTB = al.sbuf("probsTB", [128, 8, 128], BF16)
    aosb = al.sbuf("aosb", [128, DH], BF16)
    gmax8 = al.sbuf("gmax8", [128, 8], F32)
    negmax = al.sbuf("negmax", [128, 1], F32)
    rowsum = al.sbuf("rowsum", [128, 2], F32)
    rsum = al.sbuf("rsum", [128, 1], F32)
    recip = al.sbuf("recip", [128, 1], F32)

    assert nks >= 4, "P4 pipeline assumes at least 4 key slices"
    KSORD = list(range(2, nks)) + [0, 1]
    NE = nks - 2
    PEI = nks + 2

    B = snap()
    bx = [B[id(cx[0])], B[id(cx[1])]]
    bpe, bv, ba = B[id(c_pe)], B[id(c_v)], B[id(c_a)]
    btp, btq, bta = B[id(c_tp)], B[id(c_tq)], B[id(c_ta)]
    wo0_loaded = B[id(c_gw)] + 2 * 16

    if True:
        @blk.gpsimd
        def _(g):
            g.wait_ge(c_pe.sem, p3_end[id(c_pe)])
            g.wait_ge(c_v.sem, p3_end[id(c_v)])
            for dc in range(2):
                c_gw.inc(g.dma_start(wo0[:, dc, :], WO[dc * 128:(dc + 1) * 128, :]), 16)

        @blk.sync
        def _(sy):
            sy.wait_ge(c_pe.sem, p3_end[id(c_pe)])
            sy.wait_ge(cx[0].sem, bx[0])
            sy.wait_ge(cx[1].sem, bx[1])
            sy.wait_ge(cqs[0].sem, p1_end[id(cqs[0])])
            sy.wait_ge(cqs[1].sem, p1_end[id(cqs[1])])

            def qtt_load(i):
                if i >= 2:
                    sy.wait_ge(c_pe.sem, bpe + (i - 2) * PEI + nks)
                gq, tq = i // 4, i % 4
                cx[i % 2].inc(sy.dma_start(
                    qtt[i % 2][:, :, :],
                    QTS[gq, :, :, :].rearrange("f p t -> p f t")[:, :, tq * 128:(tq + 1) * 128]), 16)

            def trio(j):
                sy.wait_ge(c_a.sem, ba + j * 2 + 2)
                if j >= 1:
                    sy.wait_ge(c_pe.sem, bpe + (j - 1) * PEI + nks + 2)
                c_tq.inc(sy.dma_start(
                    probsTB[:, :, :],
                    probs[:, 0:1024].rearrange("p (c f) -> p c f", c=8),
                    transpose=True), 16)
                sy.wait_ge(c_v.sem, bv + (j + 1) * (NE + 5))
                c_ta.inc(sy.dma_start(aot[:, :, j * 128:(j + 1) * 128],
                                      aosb[:, :].rearrange("p (c f) -> p c f", c=NHC),
                                      transpose=True), 16)

            for i in range(nt_own):
                qtt_load(i)
                if i >= 1:
                    trio(i - 1)
            trio(nt_own - 1)

        @blk.tensor
        def _(t):
            t.wait_ge(c_v.sem, bv)
            for i in range(nt_own):
                t.wait_ge(cx[i % 2].sem, bx[i % 2] + (i // 2 + 1) * 16)
                if i > 0:
                    t.wait_ge(c_a.sem, ba + i * 2)
                for ksidx, ks in enumerate(KSORD):
                    if ks == 0 and i > 0:
                        t.wait_ge(c_v.sem, bv + i * (NE + 5))
                    for dc in range(NHC):
                        ins = t.matmul(ps[:, ks * 512:(ks + 1) * 512], qtt[i % 2][:, dc, :],
                                       kt[:, dc, ks * 512:(ks + 1) * 512],
                                       start=(dc == 0), stop=(dc == NHC - 1))
                    c_pe.inc(ins)
                t.wait_ge(c_tp.sem, btp + (i + 1) * 16)
                t.wait_ge(c_a.sem, ba + i * 2 + 2)
                for kc in range(8, nt_all):
                    for dhalf in range(2):
                        ins = t.matmul(ps[:, dhalf * 512:(dhalf + 1) * 512],
                                       probsTA[:, kc - 8, :],
                                       vsb[:, kc, dhalf * 512:(dhalf + 1) * 512],
                                       start=(kc == 8), stop=False)
                c_pe.inc(ins)
                t.wait_ge(c_tq.sem, btq + (i + 1) * 16)
                for kc in range(0, 8):
                    for dhalf in range(2):
                        ins = t.matmul(ps[:, dhalf * 512:(dhalf + 1) * 512],
                                       probsTB[:, kc, :],
                                       vsb[:, kc, dhalf * 512:(dhalf + 1) * 512],
                                       start=False, stop=(kc == 7))
                c_pe.inc(ins)

        @blk.vector
        def _(v):
            VPT = NE + 5
            for i in range(nt_own):
                for ksidx in range(NE):
                    v.wait_ge(c_pe.sem, bpe + i * PEI + ksidx + 1)
                    ks = KSORD[ksidx]
                    c_v.inc(v.reduce_max(gmax8[:, ksidx:ksidx + 1],
                                         ps[:, ks * 512:(ks + 1) * 512], axis=AX))
                v.wait_ge(c_v.sem, bv + i * VPT + NE)
                c_v.inc(v.reduce_max(rsum[:, :], gmax8[:, 0:NE], axis=AX))
                v.wait_ge(c_v.sem, bv + i * VPT + NE + 1)
                c_v.inc(v.tensor_scalar_mul(negmax[:, :], rsum[:, :], -1.0))
                v.wait_ge(c_a.sem, ba + i * 2 + 2)
                v.wait_ge(c_v.sem, bv + i * VPT + NE + 2)
                c_v.inc(v.tensor_add(rsum[:, :], rowsum[:, 0:1], rowsum[:, 1:2]))
                v.wait_ge(c_v.sem, bv + i * VPT + NE + 3)
                c_v.inc(v.reciprocal(recip[:, :], rsum[:, :]))
                v.wait_ge(c_pe.sem, bpe + i * PEI + nks + 2)
                v.wait_ge(c_v.sem, bv + i * VPT + NE + 4)
                if i > 0:
                    v.wait_ge(c_ta.sem, bta + i * 16)
                c_v.inc(v.tensor_scalar(aosb[:, :], ps[:, 0:DH], recip[:, 0:1], None,
                                        ALU.mult))

        @blk.scalar
        def _(sc):
            for i in range(nt_own):
                sc.wait_ge(c_v.sem, bv + i * (NE + 5) + NE + 2)
                sc.wait_ge(c_pe.sem, bpe + i * PEI + NE)
                if i > 0:
                    sc.wait_ge(c_tp.sem, btp + i * 16)
                c_a.inc(sc.activation(probs[:, 1024:nks * 512], ps[:, 1024:nks * 512],
                                      AF.Exp, bias=negmax[:, :],
                                      accum_out=rowsum[:, 0:1]))
                sc.wait_ge(c_a.sem, ba + i * 2 + 1)
                if i >= 1:
                    sc.wait_ge(c_pe.sem, bpe + (i - 1) * PEI + nks + 1)
                c_tp.inc(sc.dma_start(
                    probsTA[:, :, :],
                    probs[:, 1024:nks * 512].rearrange("p (c f) -> p c f", c=nt_all - 8),
                    transpose=True), 16)
                sc.wait_ge(c_pe.sem, bpe + i * PEI + nks)
                if i > 0:
                    sc.wait_ge(c_tq.sem, btq + i * 16)
                c_a.inc(sc.activation(probs[:, 0:1024], ps[:, 0:1024],
                                      AF.Exp, bias=negmax[:, :],
                                      accum_out=rowsum[:, 1:2]))

    al.free("recip", "rsum", "rowsum", "negmax", "gmax8", "aosb", "probsTB", "probsTA",
            "probs", "qtt1", "qtt0", "vsb", "kt")
    p4_end = snap()

    wo1 = al.sbuf("wo1", [128, 6, D], BF16)
    bo_sb = al.sbuf("bo_sb", [1, D], BF16)
    xres = [al.sbuf(f"xres{i}", [128, D], F32) for i in range(2)]
    x1b = [al.sbuf(f"x1b{i}", [128, D], F32) for i in range(2)]
    z2 = [al.sbuf(f"z2_{i}", [128, D], BF16) for i in range(2)]
    ztb = [al.sbuf(f"ztb{i}", [128, NDC, 128], BF16) for i in range(2)]
    stats5 = al.sbuf("stats5", [128, 8, 6], F32)
    mv25 = al.sbuf("mv25", [128, 2], F32)
    rstd5 = al.sbuf("rstd5", [128, 1], F32)
    rstd25 = al.sbuf("rstd25", [128, 1], F32)
    nbias5 = al.sbuf("nbias5", [128, 1], F32)
    tmp15 = al.sbuf("tmp15", [128, 1], F32)
    y25 = al.sbuf("y25", [128, 1], F32)

    B = snap()
    bgw = B[id(c_gw)]
    bx = [B[id(cx[0])], B[id(cx[1])]]
    bpe, bv, ba = B[id(c_pe)], B[id(c_v)], B[id(c_a)]
    btz = [B[id(ctz[0])], B[id(ctz[1])]]
    bsw = [B[id(csw[0])], B[id(csw[1])]]
    bsz = [B[id(csz[0])], B[id(csz[1])]]
    wo_g1_loaded = bgw + 3 * 16
    wo_g2_loaded = (cwb[0].sem, cwb[0].val + 4 * 16)
    zts_written = [(None, 0)] * nt_own

    blkg.__exit__(None, None, None)
    blkg = nc.Block()
    blk = blkg.__enter__()
    if True:
        @blk.gpsimd
        def _(g):
            g.wait_ge(c_pe.sem, p4_end[id(c_pe)])
            g.wait_ge(c_v.sem, p4_end[id(c_v)])
            g.wait_ge(c_a.sem, p4_end[id(c_a)])
            g.wait_ge(c_tp.sem, p4_end[id(c_tp)])
            g.wait_ge(c_tq.sem, p4_end[id(c_tq)])
            g.wait_ge(c_ta.sem, p4_end[id(c_ta)])
            g.wait_ge(c_gw.sem, bgw)
            c_gw.inc(g.dma_start(bo_sb[:, :], BO[:, :]), 16)
            for dc in range(2):
                c_gw.inc(g.dma_start(wo1[:, dc, :], WO[(dc + 2) * 128:(dc + 3) * 128, :]), 16)
            for dc in range(2, 6):
                cwb[0].inc(g.dma_start(wo1[:, dc, :], WO[(dc + 2) * 128:(dc + 3) * 128, :]), 16)

        @blk.sync
        def _(sy):
            sy.wait_ge(c_a.sem, p4_end[id(c_a)])
            sy.wait_ge(c_tp.sem, p4_end[id(c_tp)])
            sy.wait_ge(c_tq.sem, p4_end[id(c_tq)])
            sy.wait_ge(c_ta.sem, p4_end[id(c_ta)])
            sy.wait_ge(c_pe.sem, p4_end[id(c_pe)])
            sy.wait_ge(c_v.sem, p4_end[id(c_v)])
            sy.wait_ge(cx[0].sem, bx[0])
            sy.wait_ge(cx[1].sem, bx[1])

            def stores_of(j):
                js = j % 2
                sy.wait_ge(c_v.sem, bv + j * 9 + 2)
                csw[js].inc(sy.dma_start(X1S[j * 128:(j + 1) * 128, :], x1b[js][:, :]), 16)
                if js == 0:
                    if j >= 2:
                        sy.wait_ge(csz[0].sem, bsz[0] + (j // 2) * 16)
                    sy.wait_ge(c_a.sem, ba + j * 3 + 3)
                    ctz[0].inc(sy.dma_start(ztb[0][:, :, :],
                                            z2[0][:, :].rearrange("p (c f) -> p c f", c=NDC),
                                            transpose=True), 16)
                sy.wait_ge(ctz[js].sem, btz[js] + (j // 2 + 1) * 16)
                csz[js].inc(sy.dma_start(ZTS[j, :, :, :].rearrange("c p t -> p c t"),
                                         ztb[js][:, :, :]), 16)
                zts_written[j] = (csz[js].sem, csz[js].val)
            for i in range(nt_own):
                if i >= 2:
                    sy.wait_ge(c_v.sem, bv + (i - 2) * 9 + 2)
                cx[i % 2].inc(sy.dma_start(xres[i % 2][:, :], XB[i * 128:(i + 1) * 128, :]), 16)
                if i >= 1:
                    stores_of(i - 1)
            stores_of(nt_own - 1)

        @blk.tensor
        def _(t):
            t.wait_ge(c_gw.sem, bgw)
            t.wait_ge(c_v.sem, bv)
            t.wait_ge(c_ta.sem, p4_end[id(c_ta)])
            for i in range(nt_own):
                for half in range(2):
                    if i > 0:
                        t.wait_ge(c_v.sem, bv + (i - 1) * 9 + half + 1)
                    for dc in range(NHC):
                        wsel, wix = (wo0, dc) if dc < 2 else (wo1, dc - 2)
                        if i == 0 and half == 0 and dc == 2:
                            t.wait_ge(c_gw.sem, wo_g1_loaded)
                        if i == 0 and half == 0 and dc == 4:
                            t.wait_ge(*wo_g2_loaded)
                        for ns in range(4):
                            col = half * 2048 + ns * 512
                            t.matmul(ps[:, col:col + 512],
                                     aot[:, dc, i * 128:(i + 1) * 128],
                                     wsel[:, wix, col:col + 512],
                                     start=(dc == 0), stop=False)
                    for ns in range(4):
                        col = half * 2048 + ns * 512
                        ins = t.matmul(ps[:, col:col + 512], ones[:, 0:128],
                                       bo_sb[:, col:col + 512], start=False, stop=True)
                    c_pe.inc(ins)

        @blk.vector
        def _(v):
            for i in range(nt_own):
                s = i % 2
                v.wait_ge(cx[s].sem, bx[s] + (i // 2 + 1) * 16)
                if i >= 2:
                    v.wait_ge(csw[s].sem, bsw[s] + (i // 2) * 16)
                    v.wait_ge(c_a.sem, ba + (i - 2) * 3 + 3)
                for half in range(2):
                    v.wait_ge(c_pe.sem, bpe + i * 2 + half + 1)
                    c_v.inc(v.tensor_tensor(x1b[s][:, half * 2048:(half + 1) * 2048],
                                            ps[:, half * 2048:(half + 1) * 2048],
                                            xres[s][:, half * 2048:(half + 1) * 2048],
                                            ALU.add))
                v.wait_ge(c_v.sem, bv + i * 9 + 2)
                for k in range(8):
                    ins = v.bn_stats(stats5[:, k, :], x1b[s][:, k * 512:(k + 1) * 512])
                c_v.inc(ins)
                v.wait_ge(c_v.sem, bv + i * 9 + 3)
                c_v.inc(v.bn_aggr(mv25[:, :], stats5[:, :, :].rearrange("p c f -> p (c f)")))
                v.wait_ge(c_v.sem, bv + i * 9 + 4)
                if i > 0:
                    v.wait_ge(c_a.sem, ba + (i - 1) * 3 + 1)
                c_v.inc(v.tensor_scalar_add(tmp15[:, :], mv25[:, 1:2], EPS))
                v.wait_ge(c_a.sem, ba + i * 3 + 2)
                c_v.inc(v.tensor_mul(y25[:, :], rstd5[:, :], rstd5[:, :]))
                v.wait_ge(c_v.sem, bv + i * 9 + 6)
                c_v.inc(v.tensor_scalar(y25[:, :], y25[:, :], tmp15[:, 0:1], -0.5,
                                        ALU.mult, ALU.mult))
                v.wait_ge(c_v.sem, bv + i * 9 + 7)
                if i > 0:
                    v.wait_ge(c_a.sem, ba + (i - 1) * 3 + 3)
                c_v.inc(v.scalar_tensor_tensor(rstd25[:, :], y25[:, :], 1.5, rstd5[:, :],
                                               ALU.add, ALU.mult))
                v.wait_ge(c_v.sem, bv + i * 9 + 8)
                c_v.inc(v.tensor_scalar(nbias5[:, :], mv25[:, 0:1], rstd25[:, 0:1], -1.0,
                                        ALU.mult, ALU.mult))

        @blk.scalar
        def _(sc):
            for i in range(nt_own):
                s = i % 2
                sc.wait_ge(c_v.sem, bv + i * 9 + 5)
                c_a.inc(sc.activation(rstd5[:, :], tmp15[:, :], AF.Ln))
                sc.wait_ge(c_a.sem, ba + i * 3 + 1)
                c_a.inc(sc.activation(rstd5[:, :], rstd5[:, :], AF.Exp, scale=-0.5))
                sc.wait_ge(c_v.sem, bv + i * 9 + 9)
                if i >= 2:
                    sc.wait_ge(ctz[s].sem, btz[s] + (i // 2) * 16)
                c_a.inc(sc.activation(z2[s][:, :], x1b[s][:, :], AF.Identity,
                                      bias=nbias5[:, :], scale=rstd25[:, :]))
                if s == 1:
                    if i >= 2:
                        sc.wait_ge(csz[1].sem, bsz[1] + (i // 2) * 16)
                    sc.wait_ge(c_a.sem, ba + i * 3 + 3)
                    ctz[1].inc(sc.dma_start(ztb[1][:, :, :],
                                            z2[1][:, :].rearrange("p (c f) -> p c f", c=NDC),
                                            transpose=True), 16)

    al.free("y25", "tmp15", "nbias5", "rstd25", "rstd5", "mv25", "stats5",
            "ztb1", "ztb0", "z2_1", "z2_0", "x1b1", "x1b0", "xres1", "xres0",
            "bo_sb", "wo1", "aot", "wo0")
    p5_end = snap()

    bwp = al.sbuf("bwp", [128, D], BF16)
    bb_sb = al.sbuf("bb_sb", [1, D], BF16)
    ztl = [al.sbuf(f"ztl{i}", [128, NDC, 128], BF16) for i in range(2)]
    hwT = al.sbuf("hwT", [128, nt_own, ER], BF16)
    wb = [al.sbuf(f"wb{i}", [128, NDC, 1024], BF16) for i in range(2)]
    smw = al.sbuf("smw", [128, NDC, SMW], BF16)
    h_all = al.sbuf("h_all", [128, nt_own, ER], BF16)
    gate_all = al.sbuf("gate_all", [128, nt_own, 8], F32)
    thr_all = al.sbuf("thr_all", [128, nt_own], F32)
    gmax = al.sbuf("gmax", [128, nt_own], F32)
    gexp = al.sbuf("gexp", [128, nt_own, 8], F32)
    gsum = al.sbuf("gsum", [128, nt_own], F32)
    sig = al.sbuf("sig", [128, nt_own], F32)
    wrelu = al.sbuf("wrelu", [128, nt_own, 8], F32)
    wsum = al.sbuf("wsum", [128, nt_own], F32)
    hw = al.sbuf("hw", [128, nt_own, ER], BF16)

    B = snap()
    bgw = B[id(c_gw)]
    bx = [B[id(cx[0])], B[id(cx[1])]]
    bpe, bv = B[id(c_pe)], B[id(c_v)]
    sm_loaded = bgw + (NDC + 2) * 16
    wb_loaded = [(cwb[0].sem, cwb[0].val + NDC * 16),
                 (cwb[1].sem, cwb[1].val + NDC * 16), None, None]

    blkg.__exit__(None, None, None)
    blkg = nc.Block()
    blk = blkg.__enter__()
    if True:
        @blk.gpsimd
        def _(g):
            g.wait_ge(c_pe.sem, p5_end[id(c_pe)])
            g.wait_ge(c_a.sem, p5_end[id(c_a)])
            g.wait_ge(c_v.sem, p5_end[id(c_v)])
            g.wait_ge(ctz[0].sem, p5_end[id(ctz[0])])
            g.wait_ge(ctz[1].sem, p5_end[id(ctz[1])])
            g.wait_ge(csw[0].sem, p5_end[id(csw[0])])
            g.wait_ge(csw[1].sem, p5_end[id(csw[1])])
            g.wait_ge(csz[0].sem, p5_end[id(csz[0])])
            g.wait_ge(csz[1].sem, p5_end[id(csz[1])])
            for dck in range(NDC):
                c_gw.inc(g.dma_start(smw[:, dck, :], SMALLW[dck * 128:(dck + 1) * 128, :]), 16)
            c_gw.inc(g.dma_start(bwp[:, :], BW[:, :]), 16)
            c_gw.inc(g.dma_start(bb_sb[:, :], BBASE[:, :]), 16)
            for q in range(2):
                for dck in range(NDC):
                    cwb[q].inc(g.dma_start(
                        wb[q][:, dck, :],
                        WBT[dck * 128:(dck + 1) * 128, q * 1024:(q + 1) * 1024]), 16)

        @blk.sync
        def _(sy):
            sy.wait_ge(c_a.sem, p5_end[id(c_a)])
            sy.wait_ge(ctz[0].sem, p5_end[id(ctz[0])])
            sy.wait_ge(ctz[1].sem, p5_end[id(ctz[1])])
            sy.wait_ge(csz[0].sem, p5_end[id(csz[0])])
            sy.wait_ge(csz[1].sem, p5_end[id(csz[1])])
            sy.wait_ge(cx[0].sem, bx[0])
            sy.wait_ge(cx[1].sem, bx[1])
            for i in range(nt_own):
                s = i % 2
                if i >= 2:
                    sy.wait_ge(c_pe.sem, bpe + (i - 1))
                wsem, wval = zts_written[i]
                sy.wait_ge(wsem, wval)
                cx[s].inc(sy.dma_start(ztl[s][:, :, :],
                                       ZTS[i, :, :, :].rearrange("c p t -> p c t")), 16)

        @blk.tensor
        def _(t):
            t.wait_ge(c_gw.sem, sm_loaded)
            t.wait_ge(c_v.sem, bv)
            for i in range(nt_own):
                s = i % 2
                bank = i % 4
                col = bank * 512
                t.wait_ge(cx[s].sem, bx[s] + (i // 2 + 1) * 16)
                if i >= 4:
                    t.wait_ge(c_v.sem, bv + (i - 4) + 1)
                for dck in range(NDC):
                    t.matmul(ps[:, col:col + SMW], ztl[s][:, dck, :], smw[:, dck, :],
                             start=(dck == 0), stop=False)
                c_pe.inc(t.matmul(ps[:, col + SMW - 1:col + SMW], ones[:, 0:128],
                                  bthr_sb[:, :], start=False, stop=True))

        @blk.vector
        def _(v):
            for i in range(nt_own):
                bank = i % 4
                col = bank * 512
                v.wait_ge(c_pe.sem, bpe + i + 1)
                v.tensor_copy(h_all[:, i, :], ps[:, col:col + 128])
                v.tensor_copy(gate_all[:, i, :], ps[:, col + 128:col + 136])
                c_v.inc(v.tensor_copy(thr_all[:, i:i + 1], ps[:, col + 136:col + 137]))

    p6a_end = snap()

    B = snap()
    bv, ba, btp = B[id(c_v)], B[id(c_a)], B[id(c_tp)]

    if True:
        @blk.vector
        def _(v):
            v.wait_ge(c_v.sem, bv)
            c_v.inc(v.tensor_reduce(gmax[:, :].rearrange("p (c f) -> p c f", f=1),
                                    gate_all[:, :, :], AX, ALU.max))
            v.wait_ge(c_v.sem, bv + 1)
            c_v.inc(v.tensor_tensor(gexp[:, :, :], gate_all[:, :, :],
                                    gmax[:, :].rearrange("p (c f) -> p c f", f=1)
                                    .broadcast_to([128, nt_own, 8]),
                                    ALU.subtract))

        @blk.scalar
        def _(sc):
            sc.wait_ge(c_v.sem, bv + 2)
            c_a.inc(sc.activation(gexp[:, :, :].rearrange("p c f -> p (c f)"),
                                  gexp[:, :, :].rearrange("p c f -> p (c f)"), AF.Exp))
            sc.wait_ge(c_a.sem, ba + 1)
            c_a.inc(sc.activation(sig[:, :], thr_all[:, :], AF.Sigmoid))

        @blk.vector
        def _(v):
            v.wait_ge(c_a.sem, ba + 2)
            c_v.inc(v.tensor_reduce(gsum[:, :].rearrange("p (c f) -> p c f", f=1),
                                    gexp[:, :, :], AX, ALU.add))
            v.wait_ge(c_v.sem, bv + 3)
            c_v.inc(v.tensor_scalar_mul(gsum[:, :], gsum[:, :], -0.125))
            v.wait_ge(c_v.sem, bv + 4)
            c_v.inc(v.tensor_mul(sig[:, :], sig[:, :], gsum[:, :]))
            v.wait_ge(c_v.sem, bv + 5)
            c_v.inc(v.tensor_tensor(wrelu[:, :, :], gexp[:, :, :],
                                    sig[:, :].rearrange("p (c f) -> p c f", f=1)
                                    .broadcast_to([128, nt_own, 8]), ALU.add))
            v.wait_ge(c_v.sem, bv + 6)
            c_v.inc(v.tensor_scalar_max(wrelu[:, :, :], wrelu[:, :, :], 0.0))
            v.wait_ge(c_v.sem, bv + 7)
            c_v.inc(v.tensor_reduce(wsum[:, :].rearrange("p (c f) -> p c f", f=1),
                                    wrelu[:, :, :], AX, ALU.add))
            v.wait_ge(c_v.sem, bv + 8)
            c_v.inc(v.tensor_single_scalar(gmax[:, :], wsum[:, :], 0.0, ALU.is_equal))
            v.wait_ge(c_v.sem, bv + 9)
            c_v.inc(v.tensor_add(wsum[:, :], wsum[:, :], gmax[:, :]))
            v.wait_ge(c_v.sem, bv + 10)
            c_v.inc(v.reciprocal(gsum[:, :], wsum[:, :]))
            v.wait_ge(c_v.sem, bv + 11)
            c_v.inc(v.tensor_scalar_mul(gsum[:, :], gsum[:, :], 2.0))
            v.wait_ge(c_v.sem, bv + 12)
            c_v.inc(v.tensor_tensor(wrelu[:, :, :], wrelu[:, :, :],
                                    gsum[:, :].rearrange("p (c f) -> p c f", f=1)
                                    .broadcast_to([128, nt_own, 8]), ALU.mult))
            v.wait_ge(c_v.sem, bv + 13)
            for e in range(8):
                ins = v.tensor_tensor(hw[:, :, e * 16:(e + 1) * 16],
                                      h_all[:, :, e * 16:(e + 1) * 16],
                                      wrelu[:, :, e:e + 1].broadcast_to([128, nt_own, 16]),
                                      ALU.mult)
            c_v.inc(ins)

        @blk.sync
        def _(sy):
            sy.wait_ge(c_v.sem, bv + 14)
            c_tp.inc(sy.dma_start(hwT[:, :, :], hw[:, :, :], transpose=True), 16)

    hwT_ready = c_tp.val
    al.free("hw", "wsum", "wrelu", "sig", "gsum", "gexp", "gmax",
            "thr_all", "gate_all", "h_all", "smw")
    p6b_end = snap()

    x1l = [al.sbuf(f"x1l{i}", [128, 1024], F32) for i in range(2)]
    outb = [al.sbuf(f"outb{i}", [128, 1024], F32) for i in range(2)]

    B = snap()
    bgw = B[id(c_gw)]
    bx = [B[id(cx[0])], B[id(cx[1])]]
    bpe, bv = B[id(c_pe)], B[id(c_v)]
    bso = [B[id(cso[0])], B[id(cso[1])]]

    blkg.__exit__(None, None, None)
    blkg = nc.Block()
    blk = blkg.__enter__()
    if True:
        @blk.gpsimd
        def _(g):
            for q in range(2, 4):
                g.wait_ge(c_pe.sem, bpe + (q - 1) * nt_own)
                for dck in range(NDC):
                    ins = g.dma_start(
                        wb[q % 2][:, dck, :],
                        WBT[dck * 128:(dck + 1) * 128, q * 1024:(q + 1) * 1024])
                    cwb[q % 2].inc(ins, 16)
                wb_loaded[q] = (cwb[q % 2].sem, cwb[q % 2].val)

        @blk.sync
        def _(sy):
            sy.wait_ge(c_pe.sem, p6b_end[id(c_pe)])
            sy.wait_ge(c_v.sem, p6b_end[id(c_v)])
            sy.wait_ge(cx[0].sem, bx[0])
            sy.wait_ge(cx[1].sem, bx[1])

            def store_of(jt):
                js = jt % 2
                sy.wait_ge(c_v.sem, bv + jt + 1)
                jq, ji = jt // nt_own, jt % nt_own
                cso[js].inc(sy.dma_start(
                    OUT[ji * 128:(ji + 1) * 128, jq * 1024:(jq + 1) * 1024],
                    outb[js][:, :]), 16)
            for q in range(4):
                for i in range(nt_own):
                    it = q * nt_own + i
                    s = it % 2
                    if it >= 2:
                        sy.wait_ge(c_v.sem, bv + (it - 1))
                        sy.wait_ge(c_pe.sem, bpe + (it - 1))
                    cx[s].inc(sy.dma_start(
                        x1l[s][:, :],
                        X1S[i * 128:(i + 1) * 128, q * 1024:(q + 1) * 1024]), 16)
                    cx[s].inc(sy.dma_start(ztl[s][:, :, :],
                                           ZTS[i, :, :, :].rearrange("c p t -> p c t")), 16)
                    if it >= 1:
                        store_of(it - 1)
            store_of(4 * nt_own - 1)

        @blk.tensor
        def _(t):
            for q in range(4):
                t.wait_ge(*wb_loaded[q])
                for i in range(nt_own):
                    it = q * nt_own + i
                    s = it % 2
                    pcol = s * 1024
                    t.wait_ge(cx[s].sem, bx[s] + (it // 2 + 1) * 32)
                    if it >= 2:
                        t.wait_ge(c_v.sem, bv + (it - 1))
                    for dck in range(NDC):
                        for ns in range(2):
                            t.matmul(ps[:, pcol + ns * 512:pcol + (ns + 1) * 512],
                                     ztl[s][:, dck, :],
                                     wb[q % 2][:, dck, ns * 512:(ns + 1) * 512],
                                     start=(dck == 0), stop=False)
                    for ns in range(2):
                        t.matmul(ps[:, pcol + ns * 512:pcol + (ns + 1) * 512],
                                 ones[:, 0:128],
                                 bb_sb[:, q * 1024 + ns * 512:q * 1024 + (ns + 1) * 512],
                                 start=False, stop=False)
                    if it == 0:
                        t.wait_ge(c_tp.sem, hwT_ready)
                    for ns in range(2):
                        ins = t.matmul(ps[:, pcol + ns * 512:pcol + (ns + 1) * 512],
                                       hwT[:, i, :],
                                       bwp[:, q * 1024 + ns * 512:q * 1024 + (ns + 1) * 512],
                                       start=False, stop=True)
                    c_pe.inc(ins)

        @blk.vector
        def _(v):
            for q in range(4):
                for i in range(nt_own):
                    it = q * nt_own + i
                    s = it % 2
                    v.wait_ge(c_pe.sem, bpe + it + 1)
                    v.wait_ge(cx[s].sem, bx[s] + (it // 2 + 1) * 32)
                    if it >= 2:
                        v.wait_ge(cso[s].sem, bso[s] + (it // 2) * 16)
                    c_v.inc(v.tensor_tensor(outb[s][:, :], ps[:, s * 1024:s * 1024 + 1024],
                                            x1l[s][:, :], ALU.add))

        @blk.sync
        def _(sy):
            for c in [cso[0], cso[1], cx[0], cx[1], csw[0], csw[1],
                      csz[0], csz[1]]:
                sy.wait_ge(c.sem, c.val)

        @blk.gpsimd
        def _(g):
            g.wait_ge(c_gw.sem, c_gw.val)
            g.wait_ge(cwb[0].sem, cwb[0].val)
            g.wait_ge(cwb[1].sem, cwb[1].val)
            g.wait_ge(cgx[0].sem, cgx[0].val)
            g.wait_ge(cgx[1].sem, cgx[1].val)

        @blk.scalar
        def _(sc):
            sc.wait_ge(c_tp.sem, c_tp.val)
            sc.wait_ge(c_tq.sem, c_tq.val)
            sc.wait_ge(c_ta.sem, c_ta.val)
            sc.wait_ge(cqs[0].sem, cqs[0].val)
            sc.wait_ge(cqs[1].sem, cqs[1].val)
            sc.wait_ge(ctz[0].sem, ctz[0].val)
            sc.wait_ge(ctz[1].sem, ctz[1].val)

    blkg.__exit__(None, None, None)
    al.free_all()
    psg.__exit__(None, None, None)
    for c in ctrs:
        c.guard.__exit__(None, None, None)
    nc.finalize()
    return nc


def kernel(L0_attn_x, L0_attn_Wqkv, L0_attn_bqkv, L0_attn_Wo, L0_attn_bo,
           L0_ffn_W_base, L0_ffn_b_base, L0_ffn_W_gate, L0_ffn_W_thr,
           L0_ffn_b_thr, L0_ffn_A_w, L0_ffn_B_w):
    x = np.asarray(L0_attn_x, dtype=np.float32)
    Wqkv = np.ascontiguousarray(np.asarray(L0_attn_Wqkv, dtype=np.float32))
    bqkv = np.asarray(L0_attn_bqkv, dtype=np.float32).reshape(1, 3 * DH)
    Wo = np.ascontiguousarray(np.asarray(L0_attn_Wo, dtype=np.float32))
    bo = np.asarray(L0_attn_bo, dtype=np.float32).reshape(1, D)
    WBT = np.ascontiguousarray(np.asarray(L0_ffn_W_base, dtype=np.float32).T)
    bb = np.asarray(L0_ffn_b_base, dtype=np.float32).reshape(1, D)
    Wg = np.asarray(L0_ffn_W_gate, dtype=np.float32)
    Wt = np.asarray(L0_ffn_W_thr, dtype=np.float32)
    bt = np.asarray(L0_ffn_b_thr, dtype=np.float32).reshape(1, 1)
    Aw = np.asarray(L0_ffn_A_w, dtype=np.float32)
    Bw = np.asarray(L0_ffn_B_w, dtype=np.float32)

    SM = np.ascontiguousarray(np.concatenate(
        [Aw.reshape(ER, D).T, Wg.T, Wt.reshape(D, 1)], axis=1))
    BWP = np.ascontiguousarray(Bw.transpose(0, 2, 1).reshape(ER, D))

    n_all = x.shape[1]
    n_own = n_all // 2
    nc = build_program(n_all, n_own)

    in_maps = []
    for c in range(8):
        b, h = c // 2, c % 2
        own = x[b, h * n_own:(h + 1) * n_own]
        other = x[b, (1 - h) * n_own:(2 - h) * n_own]
        xb = np.ascontiguousarray(np.concatenate([own, other], axis=0))
        in_maps.append({
            "XB": xb, "WQKV": Wqkv, "BQKV": bqkv, "WO": Wo, "BO": bo,
            "WBT": WBT, "BBASE": bb, "SMALLW": SM, "BTHR": bt, "BW": BWP,
        })

    res = run_bass_kernel_spmd(nc, in_maps, core_ids=list(range(8)))
    out = np.empty((x.shape[0], n_all, D), dtype=np.float32)
    for c in range(8):
        b, h = c // 2, c % 2
        out[b, h * n_own:(h + 1) * n_own] = res.results[c]["OUT"]
    return out


# version 10 (final)
# speedup vs baseline: 1.1138x; 1.1138x over previous
import numpy as np
import concourse.bass as bass
import concourse.mybir as mybir
from concourse.bass_utils import run_bass_kernel_spmd

F32 = mybir.dt.float32
BF16 = mybir.dt.bfloat16
AF = mybir.ActivationFunctionType
ALU = mybir.AluOpType
AX = mybir.AxisListType.X

D = 4096
DH = 1024
NDC = 32
NHC = 8
ER = 128
SMW = 137
EPS = 1e-5
SCALE = 1.0 / 32.0


class Ctr:
    def __init__(self, nc, name):
        self.guard = nc.semaphore(name)
        self.sem = self.guard.__enter__()
        self.val = 0

    def inc(self, ins, n=1):
        ins.then_inc(self.sem, n)
        self.val += n
        return ins


class Alloc:
    def __init__(self, nc):
        self.nc = nc
        self.live = {}
        self.order = {"left": [], "right": []}

    def sbuf(self, name, shape, dt, side="right"):
        g = self.nc.sbuf_tensor(name, shape, dt, side=side)
        t = g.__enter__()
        self.live[name] = (g, side)
        self.order[side].append(name)
        return t

    def free(self, *names):
        for n in names:
            g, side = self.live.pop(n)
            assert self.order[side][-1] == n, f"non-LIFO free {n} (top={self.order[side][-1]})"
            self.order[side].pop()
            g.__exit__(None, None, None)

    def free_all(self):
        for side in ("right", "left"):
            for n in reversed(list(self.order[side])):
                g, _ = self.live.pop(n)
                g.__exit__(None, None, None)
            self.order[side] = []



def build_program(n_all, n_own):
    nt_all = n_all // 128
    nt_own = n_own // 128
    ng_all = n_all // 512
    ng_own = n_own // 512
    nks = n_all // 512

    nc = bass.Bass("TRN2", target_bir_lowering=False, debug=False)

    XB = nc.dram_tensor("XB", [n_all, D], F32, kind="ExternalInput")
    WQKV = nc.dram_tensor("WQKV", [D, 3 * DH], F32, kind="ExternalInput")
    BQKV = nc.dram_tensor("BQKV", [1, 3 * DH], F32, kind="ExternalInput")
    WO = nc.dram_tensor("WO", [DH, D], F32, kind="ExternalInput")
    BO = nc.dram_tensor("BO", [1, D], F32, kind="ExternalInput")
    WBT = nc.dram_tensor("WBT", [D, D], F32, kind="ExternalInput")
    BBASE = nc.dram_tensor("BBASE", [1, D], F32, kind="ExternalInput")
    SMALLW = nc.dram_tensor("SMALLW", [D, SMW], F32, kind="ExternalInput")
    BTHR = nc.dram_tensor("BTHR", [1, 1], F32, kind="ExternalInput")
    BW = nc.dram_tensor("BW", [ER, D], F32, kind="ExternalInput")
    OUT = nc.dram_tensor("OUT", [n_own, D], F32, kind="ExternalOutput")

    XLNT = nc.dram_tensor("XLNT", [ng_all, NDC, 128, 512], BF16, kind="Internal")
    QTS = nc.dram_tensor("QTS", [ng_own, NHC, 128, 512], BF16, kind="Internal")
    X1S = nc.dram_tensor("X1S", [n_own, D], F32, kind="Internal")
    ZTS = nc.dram_tensor("ZTS", [nt_own, NDC, 128, 128], BF16, kind="Internal")

    al = Alloc(nc)
    psg = nc.psum_tensor("ps", [128, 4096], F32)
    ps = psg.__enter__()

    c_gw = Ctr(nc, "c_gw")
    cx = [Ctr(nc, f"cx{i}") for i in range(2)]
    cqs = [Ctr(nc, f"cqs{i}") for i in range(2)]
    csw = [Ctr(nc, f"csw{i}") for i in range(2)]
    cgx = [Ctr(nc, f"cgx{i}") for i in range(2)]
    csz = [Ctr(nc, f"csz{i}") for i in range(2)]
    cso = [Ctr(nc, f"cso{i}") for i in range(2)]
    ctz = [Ctr(nc, f"ctz{i}") for i in range(2)]
    c_tp = Ctr(nc, "c_tp")
    c_tq = Ctr(nc, "c_tq")
    c_ta = Ctr(nc, "c_ta")
    cwb = [Ctr(nc, f"cwb{i}") for i in range(2)]
    c_pe = Ctr(nc, "c_pe")
    c_v = Ctr(nc, "c_v")
    c_a = Ctr(nc, "c_a")
    ctrs = [c_gw, cx[0], cx[1], cqs[0], cqs[1], csw[0], csw[1], cgx[0], cgx[1], csz[0], csz[1],
            cso[0], cso[1], ctz[0], ctz[1], c_tp, c_tq, c_ta, cwb[0], cwb[1],
            c_pe, c_v, c_a]

    def snap():
        return {id(c): c.val for c in ctrs}

    ones = al.sbuf("ones", [1, 512], BF16, side="left")
    bqkv_sb = al.sbuf("bqkv_sb", [1, 3 * DH], BF16, side="left")
    bthr_sb = al.sbuf("bthr_sb", [1, 1], BF16, side="left")
    wq = al.sbuf("wq", [128, NDC, DH], BF16, side="left")

    blkg = nc.Block()
    blk = blkg.__enter__()
    if True:
        @blk.gpsimd
        def _(g):
            g.memset(ones[:, :], 1.0)
            c_gw.inc(g.dma_start(bqkv_sb[:, :], BQKV[:, :]), 16)
            c_gw.inc(g.dma_start(bthr_sb[:, :], BTHR[:, :]), 16)
            for dck in range(NDC):
                c_gw.inc(g.dma_start(wq[:, dck, :], WQKV[dck * 128:(dck + 1) * 128, 0:DH]), 16)
    wq_loaded = c_gw.val

    qtb = al.sbuf("qtb", [128, NHC, 512], BF16)
    qtb_1 = al.sbuf("qtb_1", [128, NHC, 512], BF16)
    xbuf = [al.sbuf(f"xbuf{i}", [128, D], F32) for i in range(2)]
    zbuf = [al.sbuf(f"zbuf{i}", [128, D], BF16) for i in range(2)]
    gbuf = [al.sbuf(f"gbuf{i}", [128, NDC, 512], BF16) for i in range(2)]
    stats = [al.sbuf(f"stats{i}", [128, 8, 6], F32) for i in range(2)]
    mv2 = [al.sbuf(f"mv2_{i}", [128, 2], F32) for i in range(2)]
    rstd = [al.sbuf(f"rstd{i}", [128, 1], F32) for i in range(2)]
    rstd2 = [al.sbuf(f"rstd2{i}", [128, 1], F32) for i in range(2)]
    nbias = [al.sbuf(f"nbias{i}", [128, 1], F32) for i in range(2)]
    tmp1 = [al.sbuf(f"tmp1{i}", [128, 1], F32) for i in range(2)]
    y2 = [al.sbuf(f"y2_{i}", [128, 1], F32) for i in range(2)]

    B = snap()
    bx = [B[id(cx[0])], B[id(cx[1])]]
    bv, ba, bpe = B[id(c_v)], B[id(c_a)], B[id(c_pe)]
    bqs = [B[id(cqs[0])], B[id(cqs[1])]]
    btz = [B[id(ctz[0])], B[id(ctz[1])]]
    bsw = [B[id(cgx[0])], B[id(cgx[1])]]
    xlnt_written = [(None, 0)] * ng_all
    NT = nt_all

    def v_bn(t):
        return bv + 7 * t + 1

    def v_aggr(t):
        return bv + 7 * t + 2

    def v_eps(t):
        return bv + 7 * t + 3

    def v_y2(j):
        return bv + 7 * j + 4

    def v_nbias(j):
        return bv + 7 * j + 7

    def a_ln(t):
        return ba + 3 * t + 1

    def a_exp(t):
        return ba + 3 * t + 2

    def a_id(j):
        return ba + 3 * j + 3

    def v_tz(j):
        return btz[j % 2] + (j // 2 + 1) * 16

    qtb2 = [qtb, qtb_1]
    ev_at = {}
    for g in range(ng_own):
        j = 4 * g + 8
        ev_at.setdefault(j if j < NT else NT, []).append(g)

    V = {}
    _c = bv
    for t in range(NT + 1):
        if t >= 1:
            for k in ("y2", "t2", "rstd2", "nbias"):
                _c += 1
                V[(k, t - 1)] = _c
        if t < NT:
            for k in ("bn", "aggr", "eps"):
                _c += 1
                V[(k, t)] = _c
        for g in ev_at.get(t, []):
            for f in range(8):
                _c += 1
                V[("ev", g, f)] = _c
    p01_v_total = _c

    if True:
        @blk.sync
        def _(sy):
            def qts_store(g):
                sy.wait_ge(c_v.sem, V[("ev", g, 7)])
                c_ss.inc(sy.dma_start(QTS[g, :, :, :].rearrange("f p t -> p f t"),
                                      qtb2[g % 2][:, :, :]), 16)
            def sp_tr(j):
                sj = j % 2
                g, tl = j // 4, j % 4
                gs = g % 2
                if j >= 2:
                    sy.wait_ge(ctz[sj].sem, v_tz(j - 2))
                if tl == 0 and g >= 2:
                    sy.wait_ge(cgx[gs].sem, bsw[gs] + (g // 2) * 16)
                    if g - 2 < ng_own:
                        sy.wait_ge(c_pe.sem, bpe + (g - 1) * 8)
                sy.wait_ge(c_a.sem, a_id(j))
                ctz[sj].inc(sy.dma_start(gbuf[gs][:, :, tl * 128:(tl + 1) * 128],
                                         zbuf[sj][:, :].rearrange("p (c f) -> p c f", c=NDC),
                                         transpose=True), 16)

            for t in range(nt_all):
                s = t % 2
                if t >= 2:
                    sy.wait_ge(c_a.sem, a_id(t - 2))
                    sy.wait_ge(c_v.sem, V[("bn", t - 2)])
                cx[s].inc(sy.dma_start(xbuf[s][:, :], XB[t * 128:(t + 1) * 128, :]), 16)
                if t >= 1:
                    sp_tr(t - 1)
            sp_tr(nt_all - 1)


        @blk.vector
        def _(v):
            def newton(j):
                sj = j % 2
                v.wait_ge(c_a.sem, a_exp(j))
                c_v.inc(v.tensor_mul(y2[sj][:, :], rstd[sj][:, :], rstd[sj][:, :]))
                v.wait_ge(c_v.sem, V[("y2", j)])
                c_v.inc(v.tensor_scalar(y2[sj][:, :], y2[sj][:, :], tmp1[sj][:, 0:1], -0.5,
                                        ALU.mult, ALU.mult))
                v.wait_ge(c_v.sem, V[("t2", j)])
                if j >= 2:
                    v.wait_ge(c_a.sem, a_id(j - 2))
                c_v.inc(v.scalar_tensor_tensor(rstd2[sj][:, :], y2[sj][:, :], 1.5,
                                               rstd[sj][:, :], ALU.add, ALU.mult))
                v.wait_ge(c_v.sem, V[("rstd2", j)])
                c_v.inc(v.tensor_scalar(nbias[sj][:, :], mv2[sj][:, 0:1], rstd2[sj][:, 0:1],
                                        -1.0, ALU.mult, ALU.mult))

            def q_evict(g):
                if g >= 2:
                    v.wait_ge(cqs[g % 2].sem, bqs[g % 2] + (g // 2) * 16)
                for f in range(8):
                    v.wait_ge(c_pe.sem, bpe + g * 8 + f + 1)
                    c_v.inc(v.tensor_scalar_mul(qtb2[g % 2][:, f, :],
                                                ps[:, f * 512:(f + 1) * 512], SCALE))

            for t in range(nt_all):
                s = t % 2
                if t >= 1:
                    newton(t - 1)
                v.wait_ge(cx[s].sem, bx[s] + (t // 2 + 1) * 16)
                for i in range(8):
                    ins = v.bn_stats(stats[s][:, i, :], xbuf[s][:, i * 512:(i + 1) * 512])
                c_v.inc(ins)
                v.wait_ge(c_v.sem, V[("bn", t)])
                c_v.inc(v.bn_aggr(mv2[s][:, :], stats[s][:, :, :].rearrange("p c f -> p (c f)")))
                v.wait_ge(c_v.sem, V[("aggr", t)])
                if t >= 2:
                    v.wait_ge(c_a.sem, a_ln(t - 2))
                c_v.inc(v.tensor_scalar_add(tmp1[s][:, :], mv2[s][:, 1:2], EPS))
                for g in ev_at.get(t, []):
                    q_evict(g)
            newton(nt_all - 1)
            for g in ev_at.get(NT, []):
                q_evict(g)

        @blk.scalar
        def _(sc):
            def ident(j):
                sj = j % 2
                sc.wait_ge(c_v.sem, V[("nbias", j)])
                if j >= 2:
                    sc.wait_ge(ctz[sj].sem, v_tz(j - 2))
                c_a.inc(sc.activation(zbuf[sj][:, :], xbuf[sj][:, :], AF.Identity,
                                      bias=nbias[sj][:, :], scale=rstd2[sj][:, :]))

            def xlnt_write(g):
                gs = g % 2
                sc.wait_ge(ctz[0].sem, btz[0] + (2 * g + 2) * 16)
                sc.wait_ge(ctz[1].sem, btz[1] + (2 * g + 2) * 16)
                cgx[gs].inc(sc.dma_start(XLNT[g, :, :, :].rearrange("c p t -> p c t"),
                                         gbuf[gs][:, :, :]), 16)
                xlnt_written[g] = (cgx[gs].sem, cgx[gs].val)

            def qts_store(g):
                sc.wait_ge(c_v.sem, V[("ev", g, 7)])
                cqs[g % 2].inc(sc.dma_start(QTS[g, :, :, :].rearrange("f p t -> p f t"),
                                            qtb2[g % 2][:, :, :]), 16)

            st_due = {}
            for g in range(ng_own):
                st_due.setdefault(min(4 * g + 10, NT), []).append(g)
            for t in range(nt_all):
                s = t % 2
                if t >= 1:
                    ident(t - 1)
                    if (t - 1) % 4 == 3:
                        xlnt_write((t - 1) // 4)
                for g in st_due.get(t, []):
                    qts_store(g)
                sc.wait_ge(c_v.sem, V[("eps", t)])
                c_a.inc(sc.activation(rstd[s][:, :], tmp1[s][:, :], AF.Ln))
                sc.wait_ge(c_a.sem, a_ln(t))
                c_a.inc(sc.activation(rstd[s][:, :], rstd[s][:, :], AF.Exp, scale=-0.5))
            ident(nt_all - 1)
            xlnt_write(ng_all - 1)
            for g in st_due.get(NT, []):
                qts_store(g)

    if True:
        @blk.tensor
        def _(t):
            t.wait_ge(c_gw.sem, wq_loaded)
            for g in range(ng_own):
                t.wait_ge(ctz[0].sem, btz[0] + (2 * g + 2) * 16)
                t.wait_ge(ctz[1].sem, btz[1] + (2 * g + 2) * 16)
                if g > 0:
                    t.wait_ge(c_v.sem, V[("ev", g - 1, 7)])
                for dck in range(NDC):
                    for f in range(8):
                        t.matmul(ps[:, f * 512:(f + 1) * 512], wq[:, dck, f * 128:(f + 1) * 128],
                                 gbuf[g % 2][:, dck, :], start=(dck == 0), stop=False)
                for f in range(8):
                    c_pe.inc(t.matmul(ps[:, f * 512:(f + 1) * 512],
                                      bqkv_sb[:, f * 128:(f + 1) * 128],
                                      ones[:, :], start=False, stop=True))

    al.free("y2_1", "y2_0", "tmp11", "tmp10", "nbias1", "nbias0", "rstd21", "rstd20",
            "rstd1", "rstd0", "mv2_1", "mv2_0", "stats1", "stats0",
            "gbuf1", "gbuf0", "zbuf1", "zbuf0", "xbuf1", "xbuf0",
            "qtb_1", "qtb")
    al.free("wq")
    p1_end = snap()

    kt = al.sbuf("kt", [128, NHC, n_all], BF16, side="left")
    xg = [al.sbuf(f"xg{i}", [128, NDC, 512], BF16) for i in range(2)]
    wk = [al.sbuf(f"wk{i}", [128, NDC, 512], BF16) for i in range(2)]

    B = snap()
    bgw = B[id(c_gw)]
    bx = [B[id(cx[0])], B[id(cx[1])]]
    bpe, bv = B[id(c_pe)], B[id(c_v)]
    wk_loaded = [(cwb[0].sem, cwb[0].val + NDC * 16), (cwb[1].sem, cwb[1].val + NDC * 16)]

    blkg.__exit__(None, None, None)
    blkg = nc.Block()
    blk = blkg.__enter__()
    if True:
        @blk.gpsimd
        def _(g):
            g.wait_ge(c_pe.sem, p1_end[id(c_pe)])
            g.wait_ge(c_a.sem, p1_end[id(c_a)])
            g.wait_ge(ctz[0].sem, p1_end[id(ctz[0])])
            g.wait_ge(ctz[1].sem, p1_end[id(ctz[1])])
            g.wait_ge(cgx[0].sem, p1_end[id(cgx[0])])
            g.wait_ge(cgx[1].sem, p1_end[id(cgx[1])])
            g.wait_ge(cqs[0].sem, p1_end[id(cqs[0])])
            g.wait_ge(cqs[1].sem, p1_end[id(cqs[1])])
            for fh in range(2):
                for dck in range(NDC):
                    cwb[fh].inc(g.dma_start(
                        wk[fh][:, dck, :],
                        WQKV[dck * 128:(dck + 1) * 128, DH + fh * 512:DH + (fh + 1) * 512]), 16)

        @blk.sync
        def _(sy):
            sy.wait_ge(cx[0].sem, bx[0])
            sy.wait_ge(cx[1].sem, bx[1])
            sy.wait_ge(cqs[0].sem, p1_end[id(cqs[0])])
            sy.wait_ge(cqs[1].sem, p1_end[id(cqs[1])])
            sy.wait_ge(c_a.sem, p1_end[id(c_a)])
            sy.wait_ge(ctz[0].sem, p1_end[id(ctz[0])])
            sy.wait_ge(ctz[1].sem, p1_end[id(ctz[1])])
            sy.wait_ge(cgx[0].sem, p1_end[id(cgx[0])])
            sy.wait_ge(cgx[1].sem, p1_end[id(cgx[1])])
            for it in range(2 * ng_all):
                g = it % ng_all
                if it >= 2:
                    sy.wait_ge(c_pe.sem, bpe + (it - 1) * 4)
                cx[it % 2].inc(sy.dma_start(xg[it % 2][:, :, :],
                                            XLNT[g, :, :, :].rearrange("c p t -> p c t")), 16)

        @blk.tensor
        def _(t):
            t.wait_ge(c_v.sem, bv)
            for fh in range(2):
                t.wait_ge(*wk_loaded[fh])
                for g in range(ng_all):
                    it = fh * ng_all + g
                    t.wait_ge(cx[it % 2].sem, bx[it % 2] + (it // 2 + 1) * 16)
                    if it > 0:
                        t.wait_ge(c_v.sem, bv + it * 4)
                    for dck in range(NDC):
                        for f in range(4):
                            t.matmul(ps[:, f * 512:(f + 1) * 512],
                                     wk[fh][:, dck, f * 128:(f + 1) * 128],
                                     xg[it % 2][:, dck, :], start=(dck == 0), stop=False)
                    for f in range(4):
                        c_pe.inc(t.matmul(
                            ps[:, f * 512:(f + 1) * 512],
                            bqkv_sb[:, DH + fh * 512 + f * 128:DH + fh * 512 + (f + 1) * 128],
                            ones[:, :], start=False, stop=True))

        @blk.vector
        def _(v):
            for fh in range(2):
                for g in range(ng_all):
                    it = fh * ng_all + g
                    for f in range(4):
                        v.wait_ge(c_pe.sem, bpe + it * 4 + f + 1)
                        c_v.inc(v.tensor_copy(kt[:, fh * 4 + f, g * 512:(g + 1) * 512],
                                              ps[:, f * 512:(f + 1) * 512]))

    al.free("wk1", "wk0", "xg1", "xg0")
    p2_end = snap()

    vsb = al.sbuf("vsb", [128, nt_all, DH], BF16, side="left")
    xv = [al.sbuf(f"xv{i}", [128, NDC, 256], BF16) for i in range(2)]
    wv = al.sbuf("wv", [128, NDC, 512], BF16)

    B = snap()
    bgw = B[id(c_gw)]
    bx = [B[id(cx[0])], B[id(cx[1])]]
    bpe, bv = B[id(c_pe)], B[id(c_v)]
    wv_loaded = [bgw + NDC * 16, bgw + 2 * NDC * 16]
    pe_dhh_done = [bpe + ng_all * 4, bpe + 2 * ng_all * 4]

    blkg.__exit__(None, None, None)
    blkg = nc.Block()
    blk = blkg.__enter__()
    if True:
        @blk.gpsimd
        def _(g):
            g.wait_ge(c_pe.sem, p2_end[id(c_pe)])
            for dhh in range(2):
                if dhh == 1:
                    g.wait_ge(c_pe.sem, pe_dhh_done[0])
                for dck in range(NDC):
                    c_gw.inc(g.dma_start(
                        wv[:, dck, :],
                        WQKV[dck * 128:(dck + 1) * 128, 2 * DH + dhh * 512:2 * DH + (dhh + 1) * 512]), 16)

        @blk.sync
        def _(sy):
            sy.wait_ge(cx[0].sem, bx[0])
            sy.wait_ge(cx[1].sem, bx[1])
            sy.wait_ge(c_pe.sem, p2_end[id(c_pe)])
            for it in range(2 * ng_all):
                g = it % ng_all
                for hg in range(2):
                    if it >= 1:
                        sy.wait_ge(c_pe.sem, bpe + (it - 1) * 4 + hg * 2 + 2)
                    cx[hg].inc(sy.dma_start(
                        xv[hg][:, :, :],
                        XLNT[g, :, :, hg * 256:(hg + 1) * 256].rearrange("c p t -> p c t")), 16)

        @blk.tensor
        def _(t):
            t.wait_ge(c_v.sem, bv)
            for dhh in range(2):
                t.wait_ge(c_gw.sem, wv_loaded[dhh])
                for g in range(ng_all):
                    it = dhh * ng_all + g
                    for tl in range(4):
                        hg = tl // 2
                        bank = tl
                        t.wait_ge(cx[hg].sem, bx[hg] + (it + 1) * 16)
                        if it > 0:
                            t.wait_ge(c_v.sem, bv + (it - 1) * 4 + tl + 1)
                        for dck in range(NDC):
                            t.matmul(ps[:, bank * 512:(bank + 1) * 512],
                                     xv[hg][:, dck, (tl % 2) * 128:(tl % 2 + 1) * 128],
                                     wv[:, dck, :], start=(dck == 0), stop=False)
                        c_pe.inc(t.matmul(
                            ps[:, bank * 512:(bank + 1) * 512], ones[:, 0:128],
                            bqkv_sb[:, 2 * DH + dhh * 512:2 * DH + (dhh + 1) * 512],
                            start=False, stop=True))

        @blk.vector
        def _(v):
            for dhh in range(2):
                for g in range(ng_all):
                    it = dhh * ng_all + g
                    for tl in range(4):
                        v.wait_ge(c_pe.sem, bpe + it * 4 + tl + 1)
                        c_v.inc(v.tensor_copy(
                            vsb[:, g * 4 + tl, dhh * 512:(dhh + 1) * 512],
                            ps[:, tl * 512:(tl + 1) * 512]))

    al.free("wv", "xv1", "xv0")
    p3_end = snap()

    wo0 = al.sbuf("wo0", [128, 2, D], BF16)
    aot = al.sbuf("aot", [128, NHC, n_own], BF16)
    qtt = [al.sbuf(f"qtt{i}", [128, NHC, 128], BF16) for i in range(2)]
    probs = al.sbuf("probs", [128, n_all], BF16)
    probsTA = al.sbuf("probsTA", [128, nt_all - 8, 128], BF16)
    probsTB = al.sbuf("probsTB", [128, 8, 128], BF16)
    aosb = al.sbuf("aosb", [128, DH], BF16)
    gmax8 = al.sbuf("gmax8", [128, 8], F32)
    negmax = al.sbuf("negmax", [128, 1], F32)
    rowsum = al.sbuf("rowsum", [128, 3], F32)
    rsum = al.sbuf("rsum", [128, 1], F32)
    recip = al.sbuf("recip", [128, 1], F32)

    assert nks >= 4, "P4 pipeline assumes at least 4 key slices"
    KSORD = list(range(2, nks)) + [0, 1]
    NE = nks - 2
    NS = min(3, max(1, NE - 1))
    SA2 = NE - NS
    assert SA2 >= 1
    VPT4 = NS + 6
    PEI = nks + 3
    CA1 = 1024 + NS * 512

    B = snap()
    bx = [B[id(cx[0])], B[id(cx[1])]]
    bpe, bv, ba = B[id(c_pe)], B[id(c_v)], B[id(c_a)]
    btp, btq, bta = B[id(c_tp)], B[id(c_tq)], B[id(c_ta)]
    wo0_loaded = B[id(c_gw)] + 2 * 16

    if True:
        @blk.gpsimd
        def _(g):
            g.wait_ge(c_pe.sem, p3_end[id(c_pe)])
            g.wait_ge(c_v.sem, p3_end[id(c_v)])
            for dc in range(2):
                c_gw.inc(g.dma_start(wo0[:, dc, :], WO[dc * 128:(dc + 1) * 128, :]), 16)

        @blk.sync
        def _(sy):
            sy.wait_ge(c_pe.sem, p3_end[id(c_pe)])
            sy.wait_ge(cx[0].sem, bx[0])
            sy.wait_ge(cx[1].sem, bx[1])
            sy.wait_ge(cqs[0].sem, p1_end[id(cqs[0])])
            sy.wait_ge(cqs[1].sem, p1_end[id(cqs[1])])

            def qtt_load(i):
                if i >= 2:
                    sy.wait_ge(c_pe.sem, bpe + (i - 2) * PEI + nks)
                gq, tq = i // 4, i % 4
                cx[i % 2].inc(sy.dma_start(
                    qtt[i % 2][:, :, :],
                    QTS[gq, :, :, :].rearrange("f p t -> p f t")[:, :, tq * 128:(tq + 1) * 128]), 16)

            def trio(j):
                sy.wait_ge(c_a.sem, ba + (j + 1) * 3)
                if j >= 1:
                    sy.wait_ge(c_pe.sem, bpe + j * PEI)
                c_tq.inc(sy.dma_start(
                    probsTB[:, :, :],
                    probs[:, 0:1024].rearrange("p (c f) -> p c f", c=8),
                    transpose=True), 16)
                sy.wait_ge(c_v.sem, bv + (j + 1) * VPT4)
                c_ta.inc(sy.dma_start(aot[:, :, j * 128:(j + 1) * 128],
                                      aosb[:, :].rearrange("p (c f) -> p c f", c=NHC),
                                      transpose=True), 16)

            for i in range(nt_own):
                qtt_load(i)
                if i >= 1:
                    trio(i - 1)
            trio(nt_own - 1)

        @blk.tensor
        def _(t):
            t.wait_ge(c_v.sem, bv)
            for i in range(nt_own):
                t.wait_ge(cx[i % 2].sem, bx[i % 2] + (i // 2 + 1) * 16)
                if i > 0:
                    t.wait_ge(c_a.sem, ba + i * 3)
                for ksidx, ks in enumerate(KSORD):
                    if ks == 0 and i > 0:
                        t.wait_ge(c_v.sem, bv + i * VPT4)
                    for dc in range(NHC):
                        ins = t.matmul(ps[:, ks * 512:(ks + 1) * 512], qtt[i % 2][:, dc, :],
                                       kt[:, dc, ks * 512:(ks + 1) * 512],
                                       start=(dc == 0), stop=(dc == NHC - 1))
                    c_pe.inc(ins)
                t.wait_ge(c_tp.sem, btp + i * 32 + 16)
                t.wait_ge(c_a.sem, ba + (i + 1) * 3)
                for kc in range(8, 8 + NS * 4):
                    for dhalf in range(2):
                        ins = t.matmul(ps[:, dhalf * 512:(dhalf + 1) * 512],
                                       probsTA[:, kc - 8, :],
                                       vsb[:, kc, dhalf * 512:(dhalf + 1) * 512],
                                       start=(kc == 8), stop=False)
                c_pe.inc(ins)
                t.wait_ge(c_tp.sem, btp + (i + 1) * 32)
                for kc in range(8 + NS * 4, nt_all):
                    for dhalf in range(2):
                        ins = t.matmul(ps[:, dhalf * 512:(dhalf + 1) * 512],
                                       probsTA[:, kc - 8, :],
                                       vsb[:, kc, dhalf * 512:(dhalf + 1) * 512],
                                       start=False, stop=False)
                c_pe.inc(ins)
                t.wait_ge(c_tq.sem, btq + (i + 1) * 16)
                for kc in range(0, 8):
                    for dhalf in range(2):
                        ins = t.matmul(ps[:, dhalf * 512:(dhalf + 1) * 512],
                                       probsTB[:, kc, :],
                                       vsb[:, kc, dhalf * 512:(dhalf + 1) * 512],
                                       start=False, stop=(kc == 7))
                c_pe.inc(ins)

        @blk.vector
        def _(v):
            for i in range(nt_own):
                for ksidx in range(NS):
                    v.wait_ge(c_pe.sem, bpe + i * PEI + ksidx + 1)
                    ks = KSORD[ksidx]
                    c_v.inc(v.reduce_max(gmax8[:, ksidx:ksidx + 1],
                                         ps[:, ks * 512:(ks + 1) * 512], axis=AX))
                v.wait_ge(c_v.sem, bv + i * VPT4 + NS)
                c_v.inc(v.reduce_max(rsum[:, :], gmax8[:, 0:NS], axis=AX))
                v.wait_ge(c_v.sem, bv + i * VPT4 + NS + 1)
                c_v.inc(v.tensor_scalar_mul(negmax[:, :], rsum[:, :], -1.0))
                v.wait_ge(c_a.sem, ba + (i + 1) * 3)
                v.wait_ge(c_v.sem, bv + i * VPT4 + NS + 2)
                c_v.inc(v.tensor_add(rsum[:, :], rowsum[:, 0:1], rowsum[:, 1:2]))
                v.wait_ge(c_v.sem, bv + i * VPT4 + NS + 3)
                c_v.inc(v.tensor_add(rsum[:, :], rsum[:, :], rowsum[:, 2:3]))
                v.wait_ge(c_v.sem, bv + i * VPT4 + NS + 4)
                c_v.inc(v.reciprocal(recip[:, :], rsum[:, :]))
                v.wait_ge(c_pe.sem, bpe + i * PEI + nks + 3)
                v.wait_ge(c_v.sem, bv + i * VPT4 + NS + 5)
                if i > 0:
                    v.wait_ge(c_ta.sem, bta + i * 16)
                c_v.inc(v.tensor_scalar(aosb[:, :], ps[:, 0:DH], recip[:, 0:1], None,
                                        ALU.mult))

        @blk.scalar
        def _(sc):
            for i in range(nt_own):
                sc.wait_ge(c_v.sem, bv + i * VPT4 + NS + 2)
                sc.wait_ge(c_pe.sem, bpe + i * PEI + NS)
                if i > 0:
                    sc.wait_ge(c_tp.sem, btp + i * 32)
                c_a.inc(sc.activation(probs[:, 1024:CA1], ps[:, 1024:CA1],
                                      AF.Exp, bias=negmax[:, :],
                                      accum_out=rowsum[:, 0:1]))
                sc.wait_ge(c_a.sem, ba + i * 3 + 1)
                if i >= 1:
                    sc.wait_ge(c_pe.sem, bpe + (i - 1) * PEI + nks + 1)
                c_tp.inc(sc.dma_start(
                    probsTA[:, 0:NS * 4, :],
                    probs[:, 1024:CA1].rearrange("p (c f) -> p c f", c=NS * 4),
                    transpose=True), 16)
                sc.wait_ge(c_pe.sem, bpe + i * PEI + NE)
                c_a.inc(sc.activation(probs[:, CA1:nks * 512], ps[:, CA1:nks * 512],
                                      AF.Exp, bias=negmax[:, :],
                                      accum_out=rowsum[:, 1:2]))
                sc.wait_ge(c_a.sem, ba + i * 3 + 2)
                sc.wait_ge(c_tp.sem, btp + i * 32 + 16)
                if i >= 1:
                    sc.wait_ge(c_pe.sem, bpe + (i - 1) * PEI + nks + 2)
                c_tp.inc(sc.dma_start(
                    probsTA[:, NS * 4:nt_all - 8, :],
                    probs[:, CA1:nks * 512].rearrange("p (c f) -> p c f", c=SA2 * 4),
                    transpose=True), 16)
                sc.wait_ge(c_pe.sem, bpe + i * PEI + nks)
                if i > 0:
                    sc.wait_ge(c_tq.sem, btq + i * 16)
                c_a.inc(sc.activation(probs[:, 0:1024], ps[:, 0:1024],
                                      AF.Exp, bias=negmax[:, :],
                                      accum_out=rowsum[:, 2:3]))

    al.free("recip", "rsum", "rowsum", "negmax", "gmax8", "aosb", "probsTB", "probsTA",
            "probs", "qtt1", "qtt0", "vsb", "kt")
    p4_end = snap()

    wo1 = al.sbuf("wo1", [128, 6, D], BF16)
    bo_sb = al.sbuf("bo_sb", [1, D], BF16)
    xres = [al.sbuf(f"xres{i}", [128, D], F32) for i in range(2)]
    x1b = [al.sbuf(f"x1b{i}", [128, D], F32) for i in range(2)]
    z2 = [al.sbuf(f"z2_{i}", [128, D], BF16) for i in range(2)]
    ztb = [al.sbuf(f"ztb{i}", [128, NDC, 128], BF16) for i in range(2)]
    stats5 = al.sbuf("stats5", [128, 8, 6], F32)
    mv25 = al.sbuf("mv25", [128, 2], F32)
    rstd5 = al.sbuf("rstd5", [128, 1], F32)
    rstd25 = al.sbuf("rstd25", [128, 1], F32)
    nbias5 = al.sbuf("nbias5", [128, 1], F32)
    tmp15 = al.sbuf("tmp15", [128, 1], F32)
    y25 = al.sbuf("y25", [128, 1], F32)

    B = snap()
    bgw = B[id(c_gw)]
    bx = [B[id(cx[0])], B[id(cx[1])]]
    bpe, bv, ba = B[id(c_pe)], B[id(c_v)], B[id(c_a)]
    btz = [B[id(ctz[0])], B[id(ctz[1])]]
    bsw = [B[id(csw[0])], B[id(csw[1])]]
    bsz = [B[id(csz[0])], B[id(csz[1])]]
    wo_g1_loaded = bgw + 3 * 16
    wo_g2_loaded = (cwb[0].sem, cwb[0].val + 4 * 16)
    zts_written = [(None, 0)] * nt_own

    blkg.__exit__(None, None, None)
    blkg = nc.Block()
    blk = blkg.__enter__()
    if True:
        @blk.gpsimd
        def _(g):
            g.wait_ge(c_pe.sem, p4_end[id(c_pe)])
            g.wait_ge(c_v.sem, p4_end[id(c_v)])
            g.wait_ge(c_a.sem, p4_end[id(c_a)])
            g.wait_ge(c_tp.sem, p4_end[id(c_tp)])
            g.wait_ge(c_tq.sem, p4_end[id(c_tq)])
            g.wait_ge(c_ta.sem, p4_end[id(c_ta)])
            g.wait_ge(c_gw.sem, bgw)
            c_gw.inc(g.dma_start(bo_sb[:, :], BO[:, :]), 16)
            for dc in range(2):
                c_gw.inc(g.dma_start(wo1[:, dc, :], WO[(dc + 2) * 128:(dc + 3) * 128, :]), 16)
            for dc in range(2, 6):
                cwb[0].inc(g.dma_start(wo1[:, dc, :], WO[(dc + 2) * 128:(dc + 3) * 128, :]), 16)

        @blk.sync
        def _(sy):
            sy.wait_ge(c_a.sem, p4_end[id(c_a)])
            sy.wait_ge(c_tp.sem, p4_end[id(c_tp)])
            sy.wait_ge(c_tq.sem, p4_end[id(c_tq)])
            sy.wait_ge(c_ta.sem, p4_end[id(c_ta)])
            sy.wait_ge(c_pe.sem, p4_end[id(c_pe)])
            sy.wait_ge(c_v.sem, p4_end[id(c_v)])
            sy.wait_ge(cx[0].sem, bx[0])
            sy.wait_ge(cx[1].sem, bx[1])

            def stores_of(j):
                js = j % 2
                sy.wait_ge(c_v.sem, bv + j * 9 + 2)
                csw[js].inc(sy.dma_start(X1S[j * 128:(j + 1) * 128, :], x1b[js][:, :]), 16)
                if js == 0:
                    if j >= 2:
                        sy.wait_ge(csz[0].sem, bsz[0] + (j // 2) * 16)
                    sy.wait_ge(c_a.sem, ba + j * 3 + 3)
                    ctz[0].inc(sy.dma_start(ztb[0][:, :, :],
                                            z2[0][:, :].rearrange("p (c f) -> p c f", c=NDC),
                                            transpose=True), 16)
                sy.wait_ge(ctz[js].sem, btz[js] + (j // 2 + 1) * 16)
                csz[js].inc(sy.dma_start(ZTS[j, :, :, :].rearrange("c p t -> p c t"),
                                         ztb[js][:, :, :]), 16)
                zts_written[j] = (csz[js].sem, csz[js].val)
            for i in range(nt_own):
                if i >= 2:
                    sy.wait_ge(c_v.sem, bv + (i - 2) * 9 + 2)
                cx[i % 2].inc(sy.dma_start(xres[i % 2][:, :], XB[i * 128:(i + 1) * 128, :]), 16)
                if i >= 1:
                    stores_of(i - 1)
            stores_of(nt_own - 1)

        @blk.tensor
        def _(t):
            t.wait_ge(c_gw.sem, bgw)
            t.wait_ge(c_v.sem, bv)
            t.wait_ge(c_ta.sem, p4_end[id(c_ta)])
            for i in range(nt_own):
                for half in range(2):
                    if i > 0:
                        t.wait_ge(c_v.sem, bv + (i - 1) * 9 + half + 1)
                    for dc in range(NHC):
                        wsel, wix = (wo0, dc) if dc < 2 else (wo1, dc - 2)
                        if i == 0 and half == 0 and dc == 2:
                            t.wait_ge(c_gw.sem, wo_g1_loaded)
                        if i == 0 and half == 0 and dc == 4:
                            t.wait_ge(*wo_g2_loaded)
                        for ns in range(4):
                            col = half * 2048 + ns * 512
                            t.matmul(ps[:, col:col + 512],
                                     aot[:, dc, i * 128:(i + 1) * 128],
                                     wsel[:, wix, col:col + 512],
                                     start=(dc == 0), stop=False)
                    for ns in range(4):
                        col = half * 2048 + ns * 512
                        ins = t.matmul(ps[:, col:col + 512], ones[:, 0:128],
                                       bo_sb[:, col:col + 512], start=False, stop=True)
                    c_pe.inc(ins)

        @blk.vector
        def _(v):
            for i in range(nt_own):
                s = i % 2
                v.wait_ge(cx[s].sem, bx[s] + (i // 2 + 1) * 16)
                if i >= 2:
                    v.wait_ge(csw[s].sem, bsw[s] + (i // 2) * 16)
                    v.wait_ge(c_a.sem, ba + (i - 2) * 3 + 3)
                for half in range(2):
                    v.wait_ge(c_pe.sem, bpe + i * 2 + half + 1)
                    c_v.inc(v.tensor_tensor(x1b[s][:, half * 2048:(half + 1) * 2048],
                                            ps[:, half * 2048:(half + 1) * 2048],
                                            xres[s][:, half * 2048:(half + 1) * 2048],
                                            ALU.add))
                v.wait_ge(c_v.sem, bv + i * 9 + 2)
                for k in range(8):
                    ins = v.bn_stats(stats5[:, k, :], x1b[s][:, k * 512:(k + 1) * 512])
                c_v.inc(ins)
                v.wait_ge(c_v.sem, bv + i * 9 + 3)
                c_v.inc(v.bn_aggr(mv25[:, :], stats5[:, :, :].rearrange("p c f -> p (c f)")))
                v.wait_ge(c_v.sem, bv + i * 9 + 4)
                if i > 0:
                    v.wait_ge(c_a.sem, ba + (i - 1) * 3 + 1)
                c_v.inc(v.tensor_scalar_add(tmp15[:, :], mv25[:, 1:2], EPS))
                v.wait_ge(c_a.sem, ba + i * 3 + 2)
                c_v.inc(v.tensor_mul(y25[:, :], rstd5[:, :], rstd5[:, :]))
                v.wait_ge(c_v.sem, bv + i * 9 + 6)
                c_v.inc(v.tensor_scalar(y25[:, :], y25[:, :], tmp15[:, 0:1], -0.5,
                                        ALU.mult, ALU.mult))
                v.wait_ge(c_v.sem, bv + i * 9 + 7)
                if i > 0:
                    v.wait_ge(c_a.sem, ba + (i - 1) * 3 + 3)
                c_v.inc(v.scalar_tensor_tensor(rstd25[:, :], y25[:, :], 1.5, rstd5[:, :],
                                               ALU.add, ALU.mult))
                v.wait_ge(c_v.sem, bv + i * 9 + 8)
                c_v.inc(v.tensor_scalar(nbias5[:, :], mv25[:, 0:1], rstd25[:, 0:1], -1.0,
                                        ALU.mult, ALU.mult))

        @blk.scalar
        def _(sc):
            for i in range(nt_own):
                s = i % 2
                sc.wait_ge(c_v.sem, bv + i * 9 + 5)
                c_a.inc(sc.activation(rstd5[:, :], tmp15[:, :], AF.Ln))
                sc.wait_ge(c_a.sem, ba + i * 3 + 1)
                c_a.inc(sc.activation(rstd5[:, :], rstd5[:, :], AF.Exp, scale=-0.5))
                sc.wait_ge(c_v.sem, bv + i * 9 + 9)
                if i >= 2:
                    sc.wait_ge(ctz[s].sem, btz[s] + (i // 2) * 16)
                c_a.inc(sc.activation(z2[s][:, :], x1b[s][:, :], AF.Identity,
                                      bias=nbias5[:, :], scale=rstd25[:, :]))
                if s == 1:
                    if i >= 2:
                        sc.wait_ge(csz[1].sem, bsz[1] + (i // 2) * 16)
                    sc.wait_ge(c_a.sem, ba + i * 3 + 3)
                    ctz[1].inc(sc.dma_start(ztb[1][:, :, :],
                                            z2[1][:, :].rearrange("p (c f) -> p c f", c=NDC),
                                            transpose=True), 16)

    al.free("y25", "tmp15", "nbias5", "rstd25", "rstd5", "mv25", "stats5",
            "ztb1", "ztb0", "z2_1", "z2_0", "x1b1", "x1b0", "xres1", "xres0",
            "bo_sb", "wo1", "aot", "wo0")
    p5_end = snap()

    bwp = al.sbuf("bwp", [128, D], BF16)
    bb_sb = al.sbuf("bb_sb", [1, D], BF16)
    ztl = [al.sbuf(f"ztl{i}", [128, NDC, 128], BF16) for i in range(2)]
    hwT = al.sbuf("hwT", [128, nt_own, ER], BF16)
    wb = [al.sbuf(f"wb{i}", [128, NDC, 1024], BF16) for i in range(2)]
    smw = al.sbuf("smw", [128, NDC, SMW], BF16)
    h_all = al.sbuf("h_all", [128, nt_own, ER], BF16)
    gate_all = al.sbuf("gate_all", [128, nt_own, 8], F32)
    thr_all = al.sbuf("thr_all", [128, nt_own], F32)
    gmax = al.sbuf("gmax", [128, nt_own], F32)
    gexp = al.sbuf("gexp", [128, nt_own, 8], F32)
    gsum = al.sbuf("gsum", [128, nt_own], F32)
    sig = al.sbuf("sig", [128, nt_own], F32)
    wrelu = al.sbuf("wrelu", [128, nt_own, 8], F32)
    wsum = al.sbuf("wsum", [128, nt_own], F32)
    hw = al.sbuf("hw", [128, nt_own, ER], BF16)

    B = snap()
    bgw = B[id(c_gw)]
    bx = [B[id(cx[0])], B[id(cx[1])]]
    bpe, bv = B[id(c_pe)], B[id(c_v)]
    sm_loaded = bgw + (NDC + 2) * 16
    wb_loaded = [(cwb[0].sem, cwb[0].val + NDC * 16),
                 (cwb[1].sem, cwb[1].val + NDC * 16), None, None]

    blkg.__exit__(None, None, None)
    blkg = nc.Block()
    blk = blkg.__enter__()
    if True:
        @blk.gpsimd
        def _(g):
            g.wait_ge(c_pe.sem, p5_end[id(c_pe)])
            g.wait_ge(c_a.sem, p5_end[id(c_a)])
            g.wait_ge(c_v.sem, p5_end[id(c_v)])
            g.wait_ge(ctz[0].sem, p5_end[id(ctz[0])])
            g.wait_ge(ctz[1].sem, p5_end[id(ctz[1])])
            g.wait_ge(csw[0].sem, p5_end[id(csw[0])])
            g.wait_ge(csw[1].sem, p5_end[id(csw[1])])
            g.wait_ge(csz[0].sem, p5_end[id(csz[0])])
            g.wait_ge(csz[1].sem, p5_end[id(csz[1])])
            for dck in range(NDC):
                c_gw.inc(g.dma_start(smw[:, dck, :], SMALLW[dck * 128:(dck + 1) * 128, :]), 16)
            c_gw.inc(g.dma_start(bwp[:, :], BW[:, :]), 16)
            c_gw.inc(g.dma_start(bb_sb[:, :], BBASE[:, :]), 16)
            for q in range(2):
                for dck in range(NDC):
                    cwb[q].inc(g.dma_start(
                        wb[q][:, dck, :],
                        WBT[dck * 128:(dck + 1) * 128, q * 1024:(q + 1) * 1024]), 16)

        @blk.sync
        def _(sy):
            sy.wait_ge(c_a.sem, p5_end[id(c_a)])
            sy.wait_ge(ctz[0].sem, p5_end[id(ctz[0])])
            sy.wait_ge(ctz[1].sem, p5_end[id(ctz[1])])
            sy.wait_ge(csz[0].sem, p5_end[id(csz[0])])
            sy.wait_ge(csz[1].sem, p5_end[id(csz[1])])
            sy.wait_ge(cx[0].sem, bx[0])
            sy.wait_ge(cx[1].sem, bx[1])
            for i in range(nt_own):
                s = i % 2
                if i >= 2:
                    sy.wait_ge(c_pe.sem, bpe + (i - 1))
                wsem, wval = zts_written[i]
                sy.wait_ge(wsem, wval)
                cx[s].inc(sy.dma_start(ztl[s][:, :, :],
                                       ZTS[i, :, :, :].rearrange("c p t -> p c t")), 16)

        @blk.tensor
        def _(t):
            t.wait_ge(c_gw.sem, sm_loaded)
            t.wait_ge(c_v.sem, bv)
            for i in range(nt_own):
                s = i % 2
                bank = i % 4
                col = bank * 512
                t.wait_ge(cx[s].sem, bx[s] + (i // 2 + 1) * 16)
                if i >= 4:
                    t.wait_ge(c_v.sem, bv + (i - 4) + 1)
                for dck in range(NDC):
                    t.matmul(ps[:, col:col + SMW], ztl[s][:, dck, :], smw[:, dck, :],
                             start=(dck == 0), stop=False)
                c_pe.inc(t.matmul(ps[:, col + SMW - 1:col + SMW], ones[:, 0:128],
                                  bthr_sb[:, :], start=False, stop=True))

        @blk.vector
        def _(v):
            for i in range(nt_own):
                bank = i % 4
                col = bank * 512
                v.wait_ge(c_pe.sem, bpe + i + 1)
                v.tensor_copy(h_all[:, i, :], ps[:, col:col + 128])
                v.tensor_copy(gate_all[:, i, :], ps[:, col + 128:col + 136])
                c_v.inc(v.tensor_copy(thr_all[:, i:i + 1], ps[:, col + 136:col + 137]))

    p6a_end = snap()

    B = snap()
    bv, ba, btp = B[id(c_v)], B[id(c_a)], B[id(c_tp)]

    if True:
        @blk.vector
        def _(v):
            v.wait_ge(c_v.sem, bv)
            c_v.inc(v.tensor_reduce(gmax[:, :].rearrange("p (c f) -> p c f", f=1),
                                    gate_all[:, :, :], AX, ALU.max))
            v.wait_ge(c_v.sem, bv + 1)
            c_v.inc(v.tensor_tensor(gexp[:, :, :], gate_all[:, :, :],
                                    gmax[:, :].rearrange("p (c f) -> p c f", f=1)
                                    .broadcast_to([128, nt_own, 8]),
                                    ALU.subtract))

        @blk.scalar
        def _(sc):
            sc.wait_ge(c_v.sem, bv + 2)
            c_a.inc(sc.activation(gexp[:, :, :].rearrange("p c f -> p (c f)"),
                                  gexp[:, :, :].rearrange("p c f -> p (c f)"), AF.Exp))
            sc.wait_ge(c_a.sem, ba + 1)
            c_a.inc(sc.activation(sig[:, :], thr_all[:, :], AF.Sigmoid))

        @blk.vector
        def _(v):
            v.wait_ge(c_a.sem, ba + 2)
            c_v.inc(v.tensor_reduce(gsum[:, :].rearrange("p (c f) -> p c f", f=1),
                                    gexp[:, :, :], AX, ALU.add))
            v.wait_ge(c_v.sem, bv + 3)
            c_v.inc(v.tensor_scalar_mul(gsum[:, :], gsum[:, :], -0.125))
            v.wait_ge(c_v.sem, bv + 4)
            c_v.inc(v.tensor_mul(sig[:, :], sig[:, :], gsum[:, :]))
            v.wait_ge(c_v.sem, bv + 5)
            c_v.inc(v.tensor_tensor(wrelu[:, :, :], gexp[:, :, :],
                                    sig[:, :].rearrange("p (c f) -> p c f", f=1)
                                    .broadcast_to([128, nt_own, 8]), ALU.add))
            v.wait_ge(c_v.sem, bv + 6)
            c_v.inc(v.tensor_scalar_max(wrelu[:, :, :], wrelu[:, :, :], 0.0))
            v.wait_ge(c_v.sem, bv + 7)
            c_v.inc(v.tensor_reduce(wsum[:, :].rearrange("p (c f) -> p c f", f=1),
                                    wrelu[:, :, :], AX, ALU.add))
            v.wait_ge(c_v.sem, bv + 8)
            c_v.inc(v.tensor_single_scalar(gmax[:, :], wsum[:, :], 0.0, ALU.is_equal))
            v.wait_ge(c_v.sem, bv + 9)
            c_v.inc(v.tensor_add(wsum[:, :], wsum[:, :], gmax[:, :]))
            v.wait_ge(c_v.sem, bv + 10)
            c_v.inc(v.reciprocal(gsum[:, :], wsum[:, :]))
            v.wait_ge(c_v.sem, bv + 11)
            c_v.inc(v.tensor_scalar_mul(gsum[:, :], gsum[:, :], 2.0))
            v.wait_ge(c_v.sem, bv + 12)
            c_v.inc(v.tensor_tensor(wrelu[:, :, :], wrelu[:, :, :],
                                    gsum[:, :].rearrange("p (c f) -> p c f", f=1)
                                    .broadcast_to([128, nt_own, 8]), ALU.mult))
            v.wait_ge(c_v.sem, bv + 13)
            for e in range(8):
                ins = v.tensor_tensor(hw[:, :, e * 16:(e + 1) * 16],
                                      h_all[:, :, e * 16:(e + 1) * 16],
                                      wrelu[:, :, e:e + 1].broadcast_to([128, nt_own, 16]),
                                      ALU.mult)
            c_v.inc(ins)

        @blk.sync
        def _(sy):
            sy.wait_ge(c_v.sem, bv + 14)
            c_tp.inc(sy.dma_start(hwT[:, :, :], hw[:, :, :], transpose=True), 16)

    hwT_ready = c_tp.val
    al.free("hw", "wsum", "wrelu", "sig", "gsum", "gexp", "gmax",
            "thr_all", "gate_all", "h_all", "smw")
    p6b_end = snap()

    x1l = [al.sbuf(f"x1l{i}", [128, 1024], F32) for i in range(2)]
    outb = [al.sbuf(f"outb{i}", [128, 1024], F32) for i in range(2)]

    B = snap()
    bgw = B[id(c_gw)]
    bx = [B[id(cx[0])], B[id(cx[1])]]
    bpe, bv = B[id(c_pe)], B[id(c_v)]
    bso = [B[id(cso[0])], B[id(cso[1])]]

    blkg.__exit__(None, None, None)
    blkg = nc.Block()
    blk = blkg.__enter__()
    if True:
        @blk.gpsimd
        def _(g):
            for q in range(2, 4):
                g.wait_ge(c_pe.sem, bpe + (q - 1) * nt_own)
                for dck in range(NDC):
                    ins = g.dma_start(
                        wb[q % 2][:, dck, :],
                        WBT[dck * 128:(dck + 1) * 128, q * 1024:(q + 1) * 1024])
                    cwb[q % 2].inc(ins, 16)
                wb_loaded[q] = (cwb[q % 2].sem, cwb[q % 2].val)

        @blk.sync
        def _(sy):
            sy.wait_ge(c_pe.sem, p6b_end[id(c_pe)])
            sy.wait_ge(c_v.sem, p6b_end[id(c_v)])
            sy.wait_ge(cx[0].sem, bx[0])
            sy.wait_ge(cx[1].sem, bx[1])

            def store_of(jt):
                js = jt % 2
                sy.wait_ge(c_v.sem, bv + jt + 1)
                jq, ji = jt // nt_own, jt % nt_own
                cso[js].inc(sy.dma_start(
                    OUT[ji * 128:(ji + 1) * 128, jq * 1024:(jq + 1) * 1024],
                    outb[js][:, :]), 16)
            for q in range(4):
                for i in range(nt_own):
                    it = q * nt_own + i
                    s = it % 2
                    if it >= 2:
                        sy.wait_ge(c_v.sem, bv + (it - 1))
                        sy.wait_ge(c_pe.sem, bpe + (it - 1))
                    cx[s].inc(sy.dma_start(
                        x1l[s][:, :],
                        X1S[i * 128:(i + 1) * 128, q * 1024:(q + 1) * 1024]), 16)
                    cx[s].inc(sy.dma_start(ztl[s][:, :, :],
                                           ZTS[i, :, :, :].rearrange("c p t -> p c t")), 16)
                    if it >= 1:
                        store_of(it - 1)
            store_of(4 * nt_own - 1)

        @blk.tensor
        def _(t):
            for q in range(4):
                t.wait_ge(*wb_loaded[q])
                for i in range(nt_own):
                    it = q * nt_own + i
                    s = it % 2
                    pcol = s * 1024
                    t.wait_ge(cx[s].sem, bx[s] + (it // 2 + 1) * 32)
                    if it >= 2:
                        t.wait_ge(c_v.sem, bv + (it - 1))
                    for dck in range(NDC):
                        for ns in range(2):
                            t.matmul(ps[:, pcol + ns * 512:pcol + (ns + 1) * 512],
                                     ztl[s][:, dck, :],
                                     wb[q % 2][:, dck, ns * 512:(ns + 1) * 512],
                                     start=(dck == 0), stop=False)
                    for ns in range(2):
                        t.matmul(ps[:, pcol + ns * 512:pcol + (ns + 1) * 512],
                                 ones[:, 0:128],
                                 bb_sb[:, q * 1024 + ns * 512:q * 1024 + (ns + 1) * 512],
                                 start=False, stop=False)
                    if it == 0:
                        t.wait_ge(c_tp.sem, hwT_ready)
                    for ns in range(2):
                        ins = t.matmul(ps[:, pcol + ns * 512:pcol + (ns + 1) * 512],
                                       hwT[:, i, :],
                                       bwp[:, q * 1024 + ns * 512:q * 1024 + (ns + 1) * 512],
                                       start=False, stop=True)
                    c_pe.inc(ins)

        @blk.vector
        def _(v):
            for q in range(4):
                for i in range(nt_own):
                    it = q * nt_own + i
                    s = it % 2
                    v.wait_ge(c_pe.sem, bpe + it + 1)
                    v.wait_ge(cx[s].sem, bx[s] + (it // 2 + 1) * 32)
                    if it >= 2:
                        v.wait_ge(cso[s].sem, bso[s] + (it // 2) * 16)
                    c_v.inc(v.tensor_tensor(outb[s][:, :], ps[:, s * 1024:s * 1024 + 1024],
                                            x1l[s][:, :], ALU.add))

        @blk.sync
        def _(sy):
            for c in [cso[0], cso[1], cx[0], cx[1], csw[0], csw[1],
                      csz[0], csz[1]]:
                sy.wait_ge(c.sem, c.val)

        @blk.gpsimd
        def _(g):
            g.wait_ge(c_gw.sem, c_gw.val)
            g.wait_ge(cwb[0].sem, cwb[0].val)
            g.wait_ge(cwb[1].sem, cwb[1].val)
            g.wait_ge(cgx[0].sem, cgx[0].val)
            g.wait_ge(cgx[1].sem, cgx[1].val)

        @blk.scalar
        def _(sc):
            sc.wait_ge(c_tp.sem, c_tp.val)
            sc.wait_ge(c_tq.sem, c_tq.val)
            sc.wait_ge(c_ta.sem, c_ta.val)
            sc.wait_ge(cqs[0].sem, cqs[0].val)
            sc.wait_ge(cqs[1].sem, cqs[1].val)
            sc.wait_ge(ctz[0].sem, ctz[0].val)
            sc.wait_ge(ctz[1].sem, ctz[1].val)

    blkg.__exit__(None, None, None)
    al.free_all()
    psg.__exit__(None, None, None)
    for c in ctrs:
        c.guard.__exit__(None, None, None)
    nc.finalize()
    return nc


def kernel(L0_attn_x, L0_attn_Wqkv, L0_attn_bqkv, L0_attn_Wo, L0_attn_bo,
           L0_ffn_W_base, L0_ffn_b_base, L0_ffn_W_gate, L0_ffn_W_thr,
           L0_ffn_b_thr, L0_ffn_A_w, L0_ffn_B_w):
    x = np.asarray(L0_attn_x, dtype=np.float32)
    Wqkv = np.ascontiguousarray(np.asarray(L0_attn_Wqkv, dtype=np.float32))
    bqkv = np.asarray(L0_attn_bqkv, dtype=np.float32).reshape(1, 3 * DH)
    Wo = np.ascontiguousarray(np.asarray(L0_attn_Wo, dtype=np.float32))
    bo = np.asarray(L0_attn_bo, dtype=np.float32).reshape(1, D)
    WBT = np.ascontiguousarray(np.asarray(L0_ffn_W_base, dtype=np.float32).T)
    bb = np.asarray(L0_ffn_b_base, dtype=np.float32).reshape(1, D)
    Wg = np.asarray(L0_ffn_W_gate, dtype=np.float32)
    Wt = np.asarray(L0_ffn_W_thr, dtype=np.float32)
    bt = np.asarray(L0_ffn_b_thr, dtype=np.float32).reshape(1, 1)
    Aw = np.asarray(L0_ffn_A_w, dtype=np.float32)
    Bw = np.asarray(L0_ffn_B_w, dtype=np.float32)

    SM = np.ascontiguousarray(np.concatenate(
        [Aw.reshape(ER, D).T, Wg.T, Wt.reshape(D, 1)], axis=1))
    BWP = np.ascontiguousarray(Bw.transpose(0, 2, 1).reshape(ER, D))

    n_all = x.shape[1]
    n_own = n_all // 2
    nc = build_program(n_all, n_own)

    in_maps = []
    for c in range(8):
        b, h = c // 2, c % 2
        own = x[b, h * n_own:(h + 1) * n_own]
        other = x[b, (1 - h) * n_own:(2 - h) * n_own]
        xb = np.ascontiguousarray(np.concatenate([own, other], axis=0))
        in_maps.append({
            "XB": xb, "WQKV": Wqkv, "BQKV": bqkv, "WO": Wo, "BO": bo,
            "WBT": WBT, "BBASE": bb, "SMALLW": SM, "BTHR": bt, "BW": BWP,
        })

    res = run_bass_kernel_spmd(nc, in_maps, core_ids=list(range(8)))
    out = np.empty((x.shape[0], n_all, D), dtype=np.float32)
    for c in range(8):
        b, h = c // 2, c % 2
        out[b, h * n_own:(h + 1) * n_own] = res.results[c]["OUT"]
    return out


# version 11
# speedup vs baseline: 1.1504x; 1.0328x over previous
import numpy as np
import concourse.bass as bass
import concourse.mybir as mybir
from concourse.bass_utils import run_bass_kernel_spmd

F32 = mybir.dt.float32
BF16 = mybir.dt.bfloat16
AF = mybir.ActivationFunctionType
ALU = mybir.AluOpType
AX = mybir.AxisListType.X

D = 4096
DH = 1024
NDC = 32
NHC = 8
ER = 128
SMW = 137
EPS = 1e-5
SCALE = 1.0 / 32.0


class Ctr:
    def __init__(self, nc, name):
        self.guard = nc.semaphore(name)
        self.sem = self.guard.__enter__()
        self.val = 0

    def inc(self, ins, n=1):
        ins.then_inc(self.sem, n)
        self.val += n
        return ins


class Alloc:
    def __init__(self, nc):
        self.nc = nc
        self.live = {}
        self.order = {"left": [], "right": []}

    def sbuf(self, name, shape, dt, side="right"):
        g = self.nc.sbuf_tensor(name, shape, dt, side=side)
        t = g.__enter__()
        self.live[name] = (g, side)
        self.order[side].append(name)
        return t

    def free(self, *names):
        for n in names:
            g, side = self.live.pop(n)
            assert self.order[side][-1] == n, f"non-LIFO free {n} (top={self.order[side][-1]})"
            self.order[side].pop()
            g.__exit__(None, None, None)

    def free_all(self):
        for side in ("right", "left"):
            for n in reversed(list(self.order[side])):
                g, _ = self.live.pop(n)
                g.__exit__(None, None, None)
            self.order[side] = []



def build_program(n_all, n_own):
    nt_all = n_all // 128
    nt_own = n_own // 128
    ng_all = n_all // 512
    ng_own = n_own // 512
    nks = n_all // 512

    nc = bass.Bass("TRN2", target_bir_lowering=False, debug=False)

    XB = nc.dram_tensor("XB", [n_all, D], F32, kind="ExternalInput")
    WQKV = nc.dram_tensor("WQKV", [D, 3 * DH], F32, kind="ExternalInput")
    BQKV = nc.dram_tensor("BQKV", [1, 3 * DH], F32, kind="ExternalInput")
    WO = nc.dram_tensor("WO", [DH, D], F32, kind="ExternalInput")
    BO = nc.dram_tensor("BO", [1, D], F32, kind="ExternalInput")
    WBT = nc.dram_tensor("WBT", [D, D], F32, kind="ExternalInput")
    BBASE = nc.dram_tensor("BBASE", [1, D], F32, kind="ExternalInput")
    SMALLW = nc.dram_tensor("SMALLW", [D, SMW], F32, kind="ExternalInput")
    BTHR = nc.dram_tensor("BTHR", [1, 1], F32, kind="ExternalInput")
    BW = nc.dram_tensor("BW", [ER, D], F32, kind="ExternalInput")
    OUT = nc.dram_tensor("OUT", [n_own, D], F32, kind="ExternalOutput")

    XLNT = nc.dram_tensor("XLNT", [ng_all, NDC, 128, 512], BF16, kind="Internal")
    QTS = nc.dram_tensor("QTS", [ng_own, NHC, 128, 512], BF16, kind="Internal")
    X1S = nc.dram_tensor("X1S", [n_own, D], F32, kind="Internal")
    ZTS = nc.dram_tensor("ZTS", [nt_own, NDC, 128, 128], BF16, kind="Internal")

    al = Alloc(nc)
    psg = nc.psum_tensor("ps", [128, 4096], F32)
    ps = psg.__enter__()

    c_gw = Ctr(nc, "c_gw")
    cx = [Ctr(nc, f"cx{i}") for i in range(2)]
    cqs = [Ctr(nc, f"cqs{i}") for i in range(2)]
    csw = [Ctr(nc, f"csw{i}") for i in range(2)]
    cgx = [Ctr(nc, f"cgx{i}") for i in range(2)]
    csz = [Ctr(nc, f"csz{i}") for i in range(2)]
    cso = [Ctr(nc, f"cso{i}") for i in range(2)]
    ctz = [Ctr(nc, f"ctz{i}") for i in range(2)]
    c_tp = Ctr(nc, "c_tp")
    c_tq = Ctr(nc, "c_tq")
    c_ta = Ctr(nc, "c_ta")
    cwb = [Ctr(nc, f"cwb{i}") for i in range(2)]
    c_pe = Ctr(nc, "c_pe")
    c_v = Ctr(nc, "c_v")
    c_a = Ctr(nc, "c_a")
    ctrs = [c_gw, cx[0], cx[1], cqs[0], cqs[1], csw[0], csw[1], cgx[0], cgx[1], csz[0], csz[1],
            cso[0], cso[1], ctz[0], ctz[1], c_tp, c_tq, c_ta, cwb[0], cwb[1],
            c_pe, c_v, c_a]

    def snap():
        return {id(c): c.val for c in ctrs}

    ones = al.sbuf("ones", [1, 512], BF16, side="left")
    bqkv_sb = al.sbuf("bqkv_sb", [1, 3 * DH], BF16, side="left")
    bthr_sb = al.sbuf("bthr_sb", [1, 1], BF16, side="left")
    wq = al.sbuf("wq", [128, NDC, DH], BF16, side="left")

    blkg = nc.Block()
    blk = blkg.__enter__()
    if True:
        @blk.gpsimd
        def _(g):
            g.memset(ones[:, :], 1.0)
            c_gw.inc(g.dma_start(bqkv_sb[:, :], BQKV[:, :]), 16)
            c_gw.inc(g.dma_start(bthr_sb[:, :], BTHR[:, :]), 16)
            for dck in range(NDC):
                c_gw.inc(g.dma_start(wq[:, dck, :], WQKV[dck * 128:(dck + 1) * 128, 0:DH]), 16)
    wq_loaded = c_gw.val

    qtb = al.sbuf("qtb", [128, NHC, 512], BF16)
    qtb_1 = al.sbuf("qtb_1", [128, NHC, 512], BF16)
    xbuf = [al.sbuf(f"xbuf{i}", [128, D], F32) for i in range(2)]
    zbuf = [al.sbuf(f"zbuf{i}", [128, D], BF16) for i in range(2)]
    gbuf = [al.sbuf(f"gbuf{i}", [128, NDC, 512], BF16) for i in range(2)]
    stats = [al.sbuf(f"stats{i}", [128, 8, 6], F32) for i in range(2)]
    mv2 = [al.sbuf(f"mv2_{i}", [128, 2], F32) for i in range(2)]
    rstd = [al.sbuf(f"rstd{i}", [128, 1], F32) for i in range(2)]
    rstd2 = [al.sbuf(f"rstd2{i}", [128, 1], F32) for i in range(2)]
    nbias = [al.sbuf(f"nbias{i}", [128, 1], F32) for i in range(2)]
    tmp1 = [al.sbuf(f"tmp1{i}", [128, 1], F32) for i in range(2)]
    y2 = [al.sbuf(f"y2_{i}", [128, 1], F32) for i in range(2)]

    B = snap()
    bx = [B[id(cx[0])], B[id(cx[1])]]
    bv, ba, bpe = B[id(c_v)], B[id(c_a)], B[id(c_pe)]
    bqs = [B[id(cqs[0])], B[id(cqs[1])]]
    btz = [B[id(ctz[0])], B[id(ctz[1])]]
    bsw = [B[id(cgx[0])], B[id(cgx[1])]]
    xlnt_written = [(None, 0)] * ng_all
    NT = nt_all

    def v_bn(t):
        return bv + 7 * t + 1

    def v_aggr(t):
        return bv + 7 * t + 2

    def v_eps(t):
        return bv + 7 * t + 3

    def v_y2(j):
        return bv + 7 * j + 4

    def v_nbias(j):
        return bv + 7 * j + 7

    def a_ln(t):
        return ba + 3 * t + 1

    def a_exp(t):
        return ba + 3 * t + 2

    def a_id(j):
        return ba + 3 * j + 3

    def v_tz(j):
        return btz[j % 2] + (j // 2 + 1) * 16

    qtb2 = [qtb, qtb_1]
    ev_at = {}
    for g in range(ng_own):
        j = 4 * g + 8
        ev_at.setdefault(j if j < NT else NT, []).append(g)

    V = {}
    _c = bv
    for t in range(NT + 1):
        if t >= 1:
            for k in ("y2", "t2", "rstd2", "nbias"):
                _c += 1
                V[(k, t - 1)] = _c
        if t < NT:
            for k in ("bn", "aggr", "eps"):
                _c += 1
                V[(k, t)] = _c
        for g in ev_at.get(t, []):
            for f in range(8):
                _c += 1
                V[("ev", g, f)] = _c
    p01_v_total = _c

    if True:
        @blk.sync
        def _(sy):
            def qts_store(g):
                sy.wait_ge(c_v.sem, V[("ev", g, 7)])
                c_ss.inc(sy.dma_start(QTS[g, :, :, :].rearrange("f p t -> p f t"),
                                      qtb2[g % 2][:, :, :]), 16)
            def sp_tr(j):
                sj = j % 2
                g, tl = j // 4, j % 4
                gs = g % 2
                if j >= 2:
                    sy.wait_ge(ctz[sj].sem, v_tz(j - 2))
                if tl == 0 and g >= 2:
                    sy.wait_ge(cgx[gs].sem, bsw[gs] + (g // 2) * 16)
                    if g - 2 < ng_own:
                        sy.wait_ge(c_pe.sem, bpe + (g - 1) * 8)
                sy.wait_ge(c_a.sem, a_id(j))
                ctz[sj].inc(sy.dma_start(gbuf[gs][:, :, tl * 128:(tl + 1) * 128],
                                         zbuf[sj][:, :].rearrange("p (c f) -> p c f", c=NDC),
                                         transpose=True), 16)

            for t in range(nt_all):
                s = t % 2
                if t >= 2:
                    sy.wait_ge(c_a.sem, a_id(t - 2))
                    sy.wait_ge(c_v.sem, V[("bn", t - 2)])
                cx[s].inc(sy.dma_start(xbuf[s][:, :], XB[t * 128:(t + 1) * 128, :]), 16)
                if t >= 1:
                    sp_tr(t - 1)
            sp_tr(nt_all - 1)


        @blk.vector
        def _(v):
            def newton(j):
                sj = j % 2
                v.wait_ge(c_a.sem, a_exp(j))
                c_v.inc(v.tensor_mul(y2[sj][:, :], rstd[sj][:, :], rstd[sj][:, :]))
                v.wait_ge(c_v.sem, V[("y2", j)])
                c_v.inc(v.tensor_scalar(y2[sj][:, :], y2[sj][:, :], tmp1[sj][:, 0:1], -0.5,
                                        ALU.mult, ALU.mult))
                v.wait_ge(c_v.sem, V[("t2", j)])
                if j >= 2:
                    v.wait_ge(c_a.sem, a_id(j - 2))
                c_v.inc(v.scalar_tensor_tensor(rstd2[sj][:, :], y2[sj][:, :], 1.5,
                                               rstd[sj][:, :], ALU.add, ALU.mult))
                v.wait_ge(c_v.sem, V[("rstd2", j)])
                c_v.inc(v.tensor_scalar(nbias[sj][:, :], mv2[sj][:, 0:1], rstd2[sj][:, 0:1],
                                        -1.0, ALU.mult, ALU.mult))

            def q_evict(g):
                if g >= 2:
                    v.wait_ge(cqs[g % 2].sem, bqs[g % 2] + (g // 2) * 16)
                for f in range(8):
                    v.wait_ge(c_pe.sem, bpe + g * 8 + f + 1)
                    c_v.inc(v.tensor_scalar_mul(qtb2[g % 2][:, f, :],
                                                ps[:, f * 512:(f + 1) * 512], SCALE))

            for t in range(nt_all):
                s = t % 2
                if t >= 1:
                    newton(t - 1)
                v.wait_ge(cx[s].sem, bx[s] + (t // 2 + 1) * 16)
                for i in range(8):
                    ins = v.bn_stats(stats[s][:, i, :], xbuf[s][:, i * 512:(i + 1) * 512])
                c_v.inc(ins)
                v.wait_ge(c_v.sem, V[("bn", t)])
                c_v.inc(v.bn_aggr(mv2[s][:, :], stats[s][:, :, :].rearrange("p c f -> p (c f)")))
                v.wait_ge(c_v.sem, V[("aggr", t)])
                if t >= 2:
                    v.wait_ge(c_a.sem, a_ln(t - 2))
                c_v.inc(v.tensor_scalar_add(tmp1[s][:, :], mv2[s][:, 1:2], EPS))
                for g in ev_at.get(t, []):
                    q_evict(g)
            newton(nt_all - 1)
            for g in ev_at.get(NT, []):
                q_evict(g)

        @blk.scalar
        def _(sc):
            def ident(j):
                sj = j % 2
                sc.wait_ge(c_v.sem, V[("nbias", j)])
                if j >= 2:
                    sc.wait_ge(ctz[sj].sem, v_tz(j - 2))
                c_a.inc(sc.activation(zbuf[sj][:, :], xbuf[sj][:, :], AF.Identity,
                                      bias=nbias[sj][:, :], scale=rstd2[sj][:, :]))

            def xlnt_write(g):
                gs = g % 2
                sc.wait_ge(ctz[0].sem, btz[0] + (2 * g + 2) * 16)
                sc.wait_ge(ctz[1].sem, btz[1] + (2 * g + 2) * 16)
                cgx[gs].inc(sc.dma_start(XLNT[g, :, :, :].rearrange("c p t -> p c t"),
                                         gbuf[gs][:, :, :]), 16)
                xlnt_written[g] = (cgx[gs].sem, cgx[gs].val)

            def qts_store(g):
                sc.wait_ge(c_v.sem, V[("ev", g, 7)])
                cqs[g % 2].inc(sc.dma_start(QTS[g, :, :, :].rearrange("f p t -> p f t"),
                                            qtb2[g % 2][:, :, :]), 16)

            st_due = {}
            for g in range(ng_own):
                st_due.setdefault(min(4 * g + 10, NT), []).append(g)
            for t in range(nt_all):
                s = t % 2
                if t >= 1:
                    ident(t - 1)
                    if (t - 1) % 4 == 3:
                        xlnt_write((t - 1) // 4)
                for g in st_due.get(t, []):
                    qts_store(g)
                sc.wait_ge(c_v.sem, V[("eps", t)])
                c_a.inc(sc.activation(rstd[s][:, :], tmp1[s][:, :], AF.Ln))
                sc.wait_ge(c_a.sem, a_ln(t))
                c_a.inc(sc.activation(rstd[s][:, :], rstd[s][:, :], AF.Exp, scale=-0.5))
            ident(nt_all - 1)
            xlnt_write(ng_all - 1)
            for g in st_due.get(NT, []):
                qts_store(g)

    if True:
        @blk.tensor
        def _(t):
            t.wait_ge(c_gw.sem, wq_loaded)
            for g in range(ng_own):
                t.wait_ge(ctz[0].sem, btz[0] + (2 * g + 2) * 16)
                t.wait_ge(ctz[1].sem, btz[1] + (2 * g + 2) * 16)
                if g > 0:
                    t.wait_ge(c_v.sem, V[("ev", g - 1, 7)])
                for dck in range(NDC):
                    for f in range(8):
                        t.matmul(ps[:, f * 512:(f + 1) * 512], wq[:, dck, f * 128:(f + 1) * 128],
                                 gbuf[g % 2][:, dck, :], start=(dck == 0), stop=False)
                for f in range(8):
                    c_pe.inc(t.matmul(ps[:, f * 512:(f + 1) * 512],
                                      bqkv_sb[:, f * 128:(f + 1) * 128],
                                      ones[:, :], start=False, stop=True))

    al.free("y2_1", "y2_0", "tmp11", "tmp10", "nbias1", "nbias0", "rstd21", "rstd20",
            "rstd1", "rstd0", "mv2_1", "mv2_0", "stats1", "stats0",
            "gbuf1", "gbuf0", "zbuf1", "zbuf0", "xbuf1", "xbuf0",
            "qtb_1", "qtb")
    al.free("wq")
    p1_end = snap()

    kt = al.sbuf("kt", [128, NHC, n_all], BF16, side="left")
    xg = [al.sbuf(f"xg{i}", [128, NDC, 512], BF16) for i in range(2)]
    wk = [al.sbuf(f"wk{i}", [128, NDC, 512], BF16) for i in range(2)]

    B = snap()
    bgw = B[id(c_gw)]
    bx = [B[id(cx[0])], B[id(cx[1])]]
    bpe, bv = B[id(c_pe)], B[id(c_v)]
    wk_loaded = [(cwb[0].sem, cwb[0].val + NDC * 16), (cwb[1].sem, cwb[1].val + NDC * 16)]

    blkg.__exit__(None, None, None)
    blkg = nc.Block()
    blk = blkg.__enter__()
    if True:
        @blk.gpsimd
        def _(g):
            g.wait_ge(c_pe.sem, p1_end[id(c_pe)])
            g.wait_ge(c_a.sem, p1_end[id(c_a)])
            g.wait_ge(ctz[0].sem, p1_end[id(ctz[0])])
            g.wait_ge(ctz[1].sem, p1_end[id(ctz[1])])
            g.wait_ge(cgx[0].sem, p1_end[id(cgx[0])])
            g.wait_ge(cgx[1].sem, p1_end[id(cgx[1])])
            g.wait_ge(cqs[0].sem, p1_end[id(cqs[0])])
            g.wait_ge(cqs[1].sem, p1_end[id(cqs[1])])
            for fh in range(2):
                for dck in range(NDC):
                    cwb[fh].inc(g.dma_start(
                        wk[fh][:, dck, :],
                        WQKV[dck * 128:(dck + 1) * 128, DH + fh * 512:DH + (fh + 1) * 512]), 16)

        @blk.sync
        def _(sy):
            sy.wait_ge(cx[0].sem, bx[0])
            sy.wait_ge(cx[1].sem, bx[1])
            sy.wait_ge(cqs[0].sem, p1_end[id(cqs[0])])
            sy.wait_ge(cqs[1].sem, p1_end[id(cqs[1])])
            sy.wait_ge(c_a.sem, p1_end[id(c_a)])
            sy.wait_ge(ctz[0].sem, p1_end[id(ctz[0])])
            sy.wait_ge(ctz[1].sem, p1_end[id(ctz[1])])
            sy.wait_ge(cgx[0].sem, p1_end[id(cgx[0])])
            sy.wait_ge(cgx[1].sem, p1_end[id(cgx[1])])
            for it in range(2 * ng_all):
                g = it % ng_all
                if it >= 2:
                    sy.wait_ge(c_pe.sem, bpe + (it - 1) * 4)
                cx[it % 2].inc(sy.dma_start(xg[it % 2][:, :, :],
                                            XLNT[g, :, :, :].rearrange("c p t -> p c t")), 16)

        @blk.tensor
        def _(t):
            t.wait_ge(c_v.sem, bv)
            for fh in range(2):
                t.wait_ge(*wk_loaded[fh])
                for g in range(ng_all):
                    it = fh * ng_all + g
                    t.wait_ge(cx[it % 2].sem, bx[it % 2] + (it // 2 + 1) * 16)
                    if it > 0:
                        t.wait_ge(c_v.sem, bv + it * 4)
                    for dck in range(NDC):
                        for f in range(4):
                            t.matmul(ps[:, f * 512:(f + 1) * 512],
                                     wk[fh][:, dck, f * 128:(f + 1) * 128],
                                     xg[it % 2][:, dck, :], start=(dck == 0), stop=False)
                    for f in range(4):
                        c_pe.inc(t.matmul(
                            ps[:, f * 512:(f + 1) * 512],
                            bqkv_sb[:, DH + fh * 512 + f * 128:DH + fh * 512 + (f + 1) * 128],
                            ones[:, :], start=False, stop=True))

        @blk.vector
        def _(v):
            for fh in range(2):
                for g in range(ng_all):
                    it = fh * ng_all + g
                    for f in range(4):
                        v.wait_ge(c_pe.sem, bpe + it * 4 + f + 1)
                        c_v.inc(v.tensor_copy(kt[:, fh * 4 + f, g * 512:(g + 1) * 512],
                                              ps[:, f * 512:(f + 1) * 512]))

    al.free("wk1", "wk0", "xg1", "xg0")
    p2_end = snap()

    vsb = al.sbuf("vsb", [128, nt_all, DH], BF16, side="left")
    xv = [al.sbuf(f"xv{i}", [128, NDC, 256], BF16) for i in range(2)]
    wv = al.sbuf("wv", [128, NDC, 512], BF16)

    B = snap()
    bgw = B[id(c_gw)]
    bx = [B[id(cx[0])], B[id(cx[1])]]
    bpe, bv = B[id(c_pe)], B[id(c_v)]
    wv_loaded = [bgw + NDC * 16, bgw + 2 * NDC * 16]
    pe_dhh_done = [bpe + ng_all * 4, bpe + 2 * ng_all * 4]

    blkg.__exit__(None, None, None)
    blkg = nc.Block()
    blk = blkg.__enter__()
    if True:
        @blk.gpsimd
        def _(g):
            g.wait_ge(c_pe.sem, p2_end[id(c_pe)])
            for dhh in range(2):
                if dhh == 1:
                    g.wait_ge(c_pe.sem, pe_dhh_done[0])
                for dck in range(NDC):
                    c_gw.inc(g.dma_start(
                        wv[:, dck, :],
                        WQKV[dck * 128:(dck + 1) * 128, 2 * DH + dhh * 512:2 * DH + (dhh + 1) * 512]), 16)

        @blk.sync
        def _(sy):
            sy.wait_ge(cx[0].sem, bx[0])
            sy.wait_ge(cx[1].sem, bx[1])
            sy.wait_ge(c_pe.sem, p2_end[id(c_pe)])
            for it in range(2 * ng_all):
                g = it % ng_all
                for hg in range(2):
                    if it >= 1:
                        sy.wait_ge(c_pe.sem, bpe + (it - 1) * 4 + hg * 2 + 2)
                    cx[hg].inc(sy.dma_start(
                        xv[hg][:, :, :],
                        XLNT[g, :, :, hg * 256:(hg + 1) * 256].rearrange("c p t -> p c t")), 16)

        @blk.tensor
        def _(t):
            t.wait_ge(c_v.sem, bv)
            for dhh in range(2):
                t.wait_ge(c_gw.sem, wv_loaded[dhh])
                for g in range(ng_all):
                    it = dhh * ng_all + g
                    for tl in range(4):
                        hg = tl // 2
                        bank = tl
                        t.wait_ge(cx[hg].sem, bx[hg] + (it + 1) * 16)
                        if it > 0:
                            t.wait_ge(c_v.sem, bv + (it - 1) * 4 + tl + 1)
                        for dck in range(NDC):
                            t.matmul(ps[:, bank * 512:(bank + 1) * 512],
                                     xv[hg][:, dck, (tl % 2) * 128:(tl % 2 + 1) * 128],
                                     wv[:, dck, :], start=(dck == 0), stop=False)
                        c_pe.inc(t.matmul(
                            ps[:, bank * 512:(bank + 1) * 512], ones[:, 0:128],
                            bqkv_sb[:, 2 * DH + dhh * 512:2 * DH + (dhh + 1) * 512],
                            start=False, stop=True))

        @blk.vector
        def _(v):
            for dhh in range(2):
                for g in range(ng_all):
                    it = dhh * ng_all + g
                    for tl in range(4):
                        v.wait_ge(c_pe.sem, bpe + it * 4 + tl + 1)
                        c_v.inc(v.tensor_copy(
                            vsb[:, g * 4 + tl, dhh * 512:(dhh + 1) * 512],
                            ps[:, tl * 512:(tl + 1) * 512]))

    al.free("wv", "xv1", "xv0")
    p3_end = snap()

    wo0 = al.sbuf("wo0", [128, 2, D], BF16)
    aot = al.sbuf("aot", [128, NHC, n_own], BF16)
    qtt = [al.sbuf(f"qtt{i}", [128, NHC, 128], BF16) for i in range(2)]
    probs = al.sbuf("probs", [128, n_all], BF16)
    probsTA = al.sbuf("probsTA", [128, nt_all - 8, 128], BF16)
    probsTB = al.sbuf("probsTB", [128, 8, 128], BF16)
    aosb = al.sbuf("aosb", [128, DH], BF16)
    gmax8 = al.sbuf("gmax8", [128, 8], F32)
    negmax = al.sbuf("negmax", [128, 1], F32)
    rowsum = al.sbuf("rowsum", [128, 3], F32)
    rsum = al.sbuf("rsum", [128, 1], F32)
    recip = al.sbuf("recip", [128, 1], F32)

    assert nks >= 4, "P4 pipeline assumes at least 4 key slices"
    KSORD = list(range(2, nks)) + [0, 1]
    NE = nks - 2
    NS = min(3, max(1, NE - 1))
    SA2 = NE - NS
    assert SA2 >= 1
    VPT4 = NS + 6
    PEI = nks + 3
    CA1 = 1024 + NS * 512

    B = snap()
    bx = [B[id(cx[0])], B[id(cx[1])]]
    bpe, bv, ba = B[id(c_pe)], B[id(c_v)], B[id(c_a)]
    btp, btq, bta = B[id(c_tp)], B[id(c_tq)], B[id(c_ta)]
    wo0_loaded = B[id(c_gw)] + 2 * 16

    if True:
        @blk.gpsimd
        def _(g):
            g.wait_ge(c_pe.sem, p3_end[id(c_pe)])
            g.wait_ge(c_v.sem, p3_end[id(c_v)])
            for dc in range(2):
                c_gw.inc(g.dma_start(wo0[:, dc, :], WO[dc * 128:(dc + 1) * 128, :]), 16)

        @blk.sync
        def _(sy):
            sy.wait_ge(c_pe.sem, p3_end[id(c_pe)])
            sy.wait_ge(cx[0].sem, bx[0])
            sy.wait_ge(cx[1].sem, bx[1])
            sy.wait_ge(cqs[0].sem, p1_end[id(cqs[0])])
            sy.wait_ge(cqs[1].sem, p1_end[id(cqs[1])])

            def qtt_load(i):
                if i >= 2:
                    sy.wait_ge(c_pe.sem, bpe + (i - 2) * PEI + nks)
                gq, tq = i // 4, i % 4
                cx[i % 2].inc(sy.dma_start(
                    qtt[i % 2][:, :, :],
                    QTS[gq, :, :, :].rearrange("f p t -> p f t")[:, :, tq * 128:(tq + 1) * 128]), 16)

            def trio(j):
                sy.wait_ge(c_a.sem, ba + (j + 1) * 3)
                if j >= 1:
                    sy.wait_ge(c_pe.sem, bpe + j * PEI)
                c_tq.inc(sy.dma_start(
                    probsTB[:, :, :],
                    probs[:, 0:1024].rearrange("p (c f) -> p c f", c=8),
                    transpose=True), 16)
                sy.wait_ge(c_v.sem, bv + (j + 1) * VPT4)
                c_ta.inc(sy.dma_start(aot[:, :, j * 128:(j + 1) * 128],
                                      aosb[:, :].rearrange("p (c f) -> p c f", c=NHC),
                                      transpose=True), 16)

            for i in range(nt_own):
                qtt_load(i)
                if i >= 1:
                    trio(i - 1)
            trio(nt_own - 1)

        @blk.tensor
        def _(t):
            t.wait_ge(c_v.sem, bv)
            for i in range(nt_own):
                t.wait_ge(cx[i % 2].sem, bx[i % 2] + (i // 2 + 1) * 16)
                if i > 0:
                    t.wait_ge(c_a.sem, ba + i * 3)
                for ksidx, ks in enumerate(KSORD):
                    if ks == 0 and i > 0:
                        t.wait_ge(c_v.sem, bv + i * VPT4)
                    for dc in range(NHC):
                        ins = t.matmul(ps[:, ks * 512:(ks + 1) * 512], qtt[i % 2][:, dc, :],
                                       kt[:, dc, ks * 512:(ks + 1) * 512],
                                       start=(dc == 0), stop=(dc == NHC - 1))
                    c_pe.inc(ins)
                t.wait_ge(c_tp.sem, btp + i * 32 + 16)
                t.wait_ge(c_a.sem, ba + (i + 1) * 3)
                for kc in range(8, 8 + NS * 4):
                    for dhalf in range(2):
                        ins = t.matmul(ps[:, dhalf * 512:(dhalf + 1) * 512],
                                       probsTA[:, kc - 8, :],
                                       vsb[:, kc, dhalf * 512:(dhalf + 1) * 512],
                                       start=(kc == 8), stop=False)
                c_pe.inc(ins)
                t.wait_ge(c_tp.sem, btp + (i + 1) * 32)
                for kc in range(8 + NS * 4, nt_all):
                    for dhalf in range(2):
                        ins = t.matmul(ps[:, dhalf * 512:(dhalf + 1) * 512],
                                       probsTA[:, kc - 8, :],
                                       vsb[:, kc, dhalf * 512:(dhalf + 1) * 512],
                                       start=False, stop=False)
                c_pe.inc(ins)
                t.wait_ge(c_tq.sem, btq + (i + 1) * 16)
                for kc in range(0, 8):
                    for dhalf in range(2):
                        ins = t.matmul(ps[:, dhalf * 512:(dhalf + 1) * 512],
                                       probsTB[:, kc, :],
                                       vsb[:, kc, dhalf * 512:(dhalf + 1) * 512],
                                       start=False, stop=(kc == 7))
                c_pe.inc(ins)

        @blk.vector
        def _(v):
            for i in range(nt_own):
                for ksidx in range(NS):
                    v.wait_ge(c_pe.sem, bpe + i * PEI + ksidx + 1)
                    ks = KSORD[ksidx]
                    c_v.inc(v.reduce_max(gmax8[:, ksidx:ksidx + 1],
                                         ps[:, ks * 512:(ks + 1) * 512], axis=AX))
                v.wait_ge(c_v.sem, bv + i * VPT4 + NS)
                c_v.inc(v.reduce_max(rsum[:, :], gmax8[:, 0:NS], axis=AX))
                v.wait_ge(c_v.sem, bv + i * VPT4 + NS + 1)
                c_v.inc(v.tensor_scalar_mul(negmax[:, :], rsum[:, :], -1.0))
                v.wait_ge(c_a.sem, ba + (i + 1) * 3)
                v.wait_ge(c_v.sem, bv + i * VPT4 + NS + 2)
                c_v.inc(v.tensor_add(rsum[:, :], rowsum[:, 0:1], rowsum[:, 1:2]))
                v.wait_ge(c_v.sem, bv + i * VPT4 + NS + 3)
                c_v.inc(v.tensor_add(rsum[:, :], rsum[:, :], rowsum[:, 2:3]))
                v.wait_ge(c_v.sem, bv + i * VPT4 + NS + 4)
                c_v.inc(v.reciprocal(recip[:, :], rsum[:, :]))
                v.wait_ge(c_pe.sem, bpe + i * PEI + nks + 3)
                v.wait_ge(c_v.sem, bv + i * VPT4 + NS + 5)
                if i > 0:
                    v.wait_ge(c_ta.sem, bta + i * 16)
                c_v.inc(v.tensor_scalar(aosb[:, :], ps[:, 0:DH], recip[:, 0:1], None,
                                        ALU.mult))

        @blk.scalar
        def _(sc):
            for i in range(nt_own):
                sc.wait_ge(c_v.sem, bv + i * VPT4 + NS + 2)
                sc.wait_ge(c_pe.sem, bpe + i * PEI + NS)
                if i > 0:
                    sc.wait_ge(c_tp.sem, btp + i * 32)
                c_a.inc(sc.activation(probs[:, 1024:CA1], ps[:, 1024:CA1],
                                      AF.Exp, bias=negmax[:, :],
                                      accum_out=rowsum[:, 0:1]))
                sc.wait_ge(c_a.sem, ba + i * 3 + 1)
                if i >= 1:
                    sc.wait_ge(c_pe.sem, bpe + (i - 1) * PEI + nks + 1)
                c_tp.inc(sc.dma_start(
                    probsTA[:, 0:NS * 4, :],
                    probs[:, 1024:CA1].rearrange("p (c f) -> p c f", c=NS * 4),
                    transpose=True), 16)
                sc.wait_ge(c_pe.sem, bpe + i * PEI + NE)
                c_a.inc(sc.activation(probs[:, CA1:nks * 512], ps[:, CA1:nks * 512],
                                      AF.Exp, bias=negmax[:, :],
                                      accum_out=rowsum[:, 1:2]))
                sc.wait_ge(c_a.sem, ba + i * 3 + 2)
                sc.wait_ge(c_tp.sem, btp + i * 32 + 16)
                if i >= 1:
                    sc.wait_ge(c_pe.sem, bpe + (i - 1) * PEI + nks + 2)
                c_tp.inc(sc.dma_start(
                    probsTA[:, NS * 4:nt_all - 8, :],
                    probs[:, CA1:nks * 512].rearrange("p (c f) -> p c f", c=SA2 * 4),
                    transpose=True), 16)
                sc.wait_ge(c_pe.sem, bpe + i * PEI + nks)
                if i > 0:
                    sc.wait_ge(c_tq.sem, btq + i * 16)
                c_a.inc(sc.activation(probs[:, 0:1024], ps[:, 0:1024],
                                      AF.Exp, bias=negmax[:, :],
                                      accum_out=rowsum[:, 2:3]))

    al.free("recip", "rsum", "rowsum", "negmax", "gmax8", "aosb", "probsTB", "probsTA",
            "probs", "qtt1", "qtt0", "vsb", "kt")
    p4_end = snap()

    wo1 = al.sbuf("wo1", [128, 6, D], BF16)
    bo_sb = al.sbuf("bo_sb", [1, D], BF16)
    xres = [al.sbuf(f"xres{i}", [128, D], F32) for i in range(2)]
    x1b = [al.sbuf(f"x1b{i}", [128, D], F32) for i in range(2)]
    z2 = [al.sbuf(f"z2_{i}", [128, D], BF16) for i in range(2)]
    ztb = [al.sbuf(f"ztb{i}", [128, NDC, 128], BF16) for i in range(2)]
    stats5 = al.sbuf("stats5", [128, 8, 6], F32)
    mv25 = al.sbuf("mv25", [128, 2], F32)
    rstd5 = al.sbuf("rstd5", [128, 1], F32)
    rstd25 = al.sbuf("rstd25", [128, 1], F32)
    nbias5 = al.sbuf("nbias5", [128, 1], F32)
    tmp15 = al.sbuf("tmp15", [128, 1], F32)
    y25 = al.sbuf("y25", [128, 1], F32)

    B = snap()
    bgw = B[id(c_gw)]
    bx = [B[id(cx[0])], B[id(cx[1])]]
    bpe, bv, ba = B[id(c_pe)], B[id(c_v)], B[id(c_a)]
    btz = [B[id(ctz[0])], B[id(ctz[1])]]
    bsw = [B[id(csw[0])], B[id(csw[1])]]
    bsz = [B[id(csz[0])], B[id(csz[1])]]
    wo_g1_loaded = bgw + 3 * 16
    wo_g2_loaded = (cwb[0].sem, cwb[0].val + 4 * 16)
    zts_written = [(None, 0)] * nt_own

    blkg.__exit__(None, None, None)
    blkg = nc.Block()
    blk = blkg.__enter__()
    if True:
        @blk.gpsimd
        def _(g):
            g.wait_ge(c_pe.sem, p4_end[id(c_pe)])
            g.wait_ge(c_v.sem, p4_end[id(c_v)])
            g.wait_ge(c_a.sem, p4_end[id(c_a)])
            g.wait_ge(c_tp.sem, p4_end[id(c_tp)])
            g.wait_ge(c_tq.sem, p4_end[id(c_tq)])
            g.wait_ge(c_ta.sem, p4_end[id(c_ta)])
            g.wait_ge(c_gw.sem, bgw)
            c_gw.inc(g.dma_start(bo_sb[:, :], BO[:, :]), 16)
            for dc in range(2):
                c_gw.inc(g.dma_start(wo1[:, dc, :], WO[(dc + 2) * 128:(dc + 3) * 128, :]), 16)
            for dc in range(2, 6):
                cwb[0].inc(g.dma_start(wo1[:, dc, :], WO[(dc + 2) * 128:(dc + 3) * 128, :]), 16)

        @blk.sync
        def _(sy):
            sy.wait_ge(c_a.sem, p4_end[id(c_a)])
            sy.wait_ge(c_tp.sem, p4_end[id(c_tp)])
            sy.wait_ge(c_tq.sem, p4_end[id(c_tq)])
            sy.wait_ge(c_ta.sem, p4_end[id(c_ta)])
            sy.wait_ge(c_pe.sem, p4_end[id(c_pe)])
            sy.wait_ge(c_v.sem, p4_end[id(c_v)])
            sy.wait_ge(cx[0].sem, bx[0])
            sy.wait_ge(cx[1].sem, bx[1])

            def stores_of(j):
                js = j % 2
                sy.wait_ge(c_v.sem, bv + j * 9 + 2)
                csw[js].inc(sy.dma_start(X1S[j * 128:(j + 1) * 128, :], x1b[js][:, :]), 16)
                if js == 0:
                    if j >= 2:
                        sy.wait_ge(csz[0].sem, bsz[0] + (j // 2) * 16)
                    sy.wait_ge(c_a.sem, ba + j * 3 + 3)
                    ctz[0].inc(sy.dma_start(ztb[0][:, :, :],
                                            z2[0][:, :].rearrange("p (c f) -> p c f", c=NDC),
                                            transpose=True), 16)
                sy.wait_ge(ctz[js].sem, btz[js] + (j // 2 + 1) * 16)
                csz[js].inc(sy.dma_start(ZTS[j, :, :, :].rearrange("c p t -> p c t"),
                                         ztb[js][:, :, :]), 16)
                zts_written[j] = (csz[js].sem, csz[js].val)
            for i in range(nt_own):
                if i >= 2:
                    sy.wait_ge(c_v.sem, bv + (i - 2) * 9 + 2)
                cx[i % 2].inc(sy.dma_start(xres[i % 2][:, :], XB[i * 128:(i + 1) * 128, :]), 16)
                if i >= 1:
                    stores_of(i - 1)
            stores_of(nt_own - 1)

        @blk.tensor
        def _(t):
            t.wait_ge(c_gw.sem, bgw)
            t.wait_ge(c_v.sem, bv)
            t.wait_ge(c_ta.sem, p4_end[id(c_ta)])
            for i in range(nt_own):
                for half in range(2):
                    if i > 0:
                        t.wait_ge(c_v.sem, bv + (i - 1) * 9 + half + 1)
                    for dc in range(NHC):
                        wsel, wix = (wo0, dc) if dc < 2 else (wo1, dc - 2)
                        if i == 0 and half == 0 and dc == 2:
                            t.wait_ge(c_gw.sem, wo_g1_loaded)
                        if i == 0 and half == 0 and dc == 4:
                            t.wait_ge(*wo_g2_loaded)
                        for ns in range(4):
                            col = half * 2048 + ns * 512
                            t.matmul(ps[:, col:col + 512],
                                     aot[:, dc, i * 128:(i + 1) * 128],
                                     wsel[:, wix, col:col + 512],
                                     start=(dc == 0), stop=False)
                    for ns in range(4):
                        col = half * 2048 + ns * 512
                        ins = t.matmul(ps[:, col:col + 512], ones[:, 0:128],
                                       bo_sb[:, col:col + 512], start=False, stop=True)
                    c_pe.inc(ins)

        @blk.vector
        def _(v):
            for i in range(nt_own):
                s = i % 2
                v.wait_ge(cx[s].sem, bx[s] + (i // 2 + 1) * 16)
                if i >= 2:
                    v.wait_ge(csw[s].sem, bsw[s] + (i // 2) * 16)
                    v.wait_ge(c_a.sem, ba + (i - 2) * 3 + 3)
                for half in range(2):
                    v.wait_ge(c_pe.sem, bpe + i * 2 + half + 1)
                    c_v.inc(v.tensor_tensor(x1b[s][:, half * 2048:(half + 1) * 2048],
                                            ps[:, half * 2048:(half + 1) * 2048],
                                            xres[s][:, half * 2048:(half + 1) * 2048],
                                            ALU.add))
                v.wait_ge(c_v.sem, bv + i * 9 + 2)
                for k in range(8):
                    ins = v.bn_stats(stats5[:, k, :], x1b[s][:, k * 512:(k + 1) * 512])
                c_v.inc(ins)
                v.wait_ge(c_v.sem, bv + i * 9 + 3)
                c_v.inc(v.bn_aggr(mv25[:, :], stats5[:, :, :].rearrange("p c f -> p (c f)")))
                v.wait_ge(c_v.sem, bv + i * 9 + 4)
                if i > 0:
                    v.wait_ge(c_a.sem, ba + (i - 1) * 3 + 1)
                c_v.inc(v.tensor_scalar_add(tmp15[:, :], mv25[:, 1:2], EPS))
                v.wait_ge(c_a.sem, ba + i * 3 + 2)
                c_v.inc(v.tensor_mul(y25[:, :], rstd5[:, :], rstd5[:, :]))
                v.wait_ge(c_v.sem, bv + i * 9 + 6)
                c_v.inc(v.tensor_scalar(y25[:, :], y25[:, :], tmp15[:, 0:1], -0.5,
                                        ALU.mult, ALU.mult))
                v.wait_ge(c_v.sem, bv + i * 9 + 7)
                if i > 0:
                    v.wait_ge(c_a.sem, ba + (i - 1) * 3 + 3)
                c_v.inc(v.scalar_tensor_tensor(rstd25[:, :], y25[:, :], 1.5, rstd5[:, :],
                                               ALU.add, ALU.mult))
                v.wait_ge(c_v.sem, bv + i * 9 + 8)
                c_v.inc(v.tensor_scalar(nbias5[:, :], mv25[:, 0:1], rstd25[:, 0:1], -1.0,
                                        ALU.mult, ALU.mult))

        @blk.scalar
        def _(sc):
            for i in range(nt_own):
                s = i % 2
                sc.wait_ge(c_v.sem, bv + i * 9 + 5)
                c_a.inc(sc.activation(rstd5[:, :], tmp15[:, :], AF.Ln))
                sc.wait_ge(c_a.sem, ba + i * 3 + 1)
                c_a.inc(sc.activation(rstd5[:, :], rstd5[:, :], AF.Exp, scale=-0.5))
                sc.wait_ge(c_v.sem, bv + i * 9 + 9)
                if i >= 2:
                    sc.wait_ge(ctz[s].sem, btz[s] + (i // 2) * 16)
                c_a.inc(sc.activation(z2[s][:, :], x1b[s][:, :], AF.Identity,
                                      bias=nbias5[:, :], scale=rstd25[:, :]))
                if s == 1:
                    if i >= 2:
                        sc.wait_ge(csz[1].sem, bsz[1] + (i // 2) * 16)
                    sc.wait_ge(c_a.sem, ba + i * 3 + 3)
                    ctz[1].inc(sc.dma_start(ztb[1][:, :, :],
                                            z2[1][:, :].rearrange("p (c f) -> p c f", c=NDC),
                                            transpose=True), 16)

    al.free("y25", "tmp15", "nbias5", "rstd25", "rstd5", "mv25", "stats5",
            "ztb1", "ztb0", "z2_1", "z2_0", "x1b1", "x1b0", "xres1", "xres0",
            "bo_sb", "wo1", "aot", "wo0")
    p5_end = snap()

    bwp = al.sbuf("bwp", [128, D], BF16)
    bb_sb = al.sbuf("bb_sb", [1, D], BF16)
    ztl = [al.sbuf(f"ztl{i}", [128, NDC, 128], BF16) for i in range(2)]
    hwT = al.sbuf("hwT", [128, nt_own, ER], BF16)
    wb = [al.sbuf(f"wb{i}", [128, NDC, 1024], BF16) for i in range(2)]
    smw = al.sbuf("smw", [128, NDC, SMW], BF16)
    h_all = al.sbuf("h_all", [128, nt_own, ER], BF16)
    gate_all = al.sbuf("gate_all", [128, nt_own, 8], F32)
    thr_all = al.sbuf("thr_all", [128, nt_own], F32)
    gmax = al.sbuf("gmax", [128, nt_own], F32)
    gexp = al.sbuf("gexp", [128, nt_own, 8], F32)
    gsum = al.sbuf("gsum", [128, nt_own], F32)
    sig = al.sbuf("sig", [128, nt_own], F32)
    wrelu = al.sbuf("wrelu", [128, nt_own, 8], F32)
    wsum = al.sbuf("wsum", [128, nt_own], F32)
    hw = al.sbuf("hw", [128, nt_own, ER], BF16)

    B = snap()
    bgw = B[id(c_gw)]
    bx = [B[id(cx[0])], B[id(cx[1])]]
    bpe, bv = B[id(c_pe)], B[id(c_v)]
    sm_loaded = bgw + (NDC + 2) * 16
    wb_loaded = [(cwb[0].sem, cwb[0].val + NDC * 16),
                 (cwb[1].sem, cwb[1].val + NDC * 16), None, None]

    blkg.__exit__(None, None, None)
    blkg = nc.Block()
    blk = blkg.__enter__()
    if True:
        @blk.gpsimd
        def _(g):
            g.wait_ge(c_pe.sem, p5_end[id(c_pe)])
            g.wait_ge(c_a.sem, p5_end[id(c_a)])
            g.wait_ge(c_v.sem, p5_end[id(c_v)])
            g.wait_ge(ctz[0].sem, p5_end[id(ctz[0])])
            g.wait_ge(ctz[1].sem, p5_end[id(ctz[1])])
            g.wait_ge(csw[0].sem, p5_end[id(csw[0])])
            g.wait_ge(csw[1].sem, p5_end[id(csw[1])])
            g.wait_ge(csz[0].sem, p5_end[id(csz[0])])
            g.wait_ge(csz[1].sem, p5_end[id(csz[1])])
            for dck in range(NDC):
                c_gw.inc(g.dma_start(smw[:, dck, :], SMALLW[dck * 128:(dck + 1) * 128, :]), 16)
            c_gw.inc(g.dma_start(bwp[:, :], BW[:, :]), 16)
            c_gw.inc(g.dma_start(bb_sb[:, :], BBASE[:, :]), 16)
            for q in range(2):
                for dck in range(NDC):
                    g.wait_ge(c_pe.sem, bpe + min((q * NDC + dck) // 4, nt_own - 1))
                    cwb[q].inc(g.dma_start(
                        wb[q][:, dck, :],
                        WBT[dck * 128:(dck + 1) * 128, q * 1024:(q + 1) * 1024]), 16)

        @blk.sync
        def _(sy):
            sy.wait_ge(c_a.sem, p5_end[id(c_a)])
            sy.wait_ge(ctz[0].sem, p5_end[id(ctz[0])])
            sy.wait_ge(ctz[1].sem, p5_end[id(ctz[1])])
            sy.wait_ge(csz[0].sem, p5_end[id(csz[0])])
            sy.wait_ge(csz[1].sem, p5_end[id(csz[1])])
            sy.wait_ge(cx[0].sem, bx[0])
            sy.wait_ge(cx[1].sem, bx[1])
            for i in range(nt_own):
                s = i % 2
                if i >= 2:
                    sy.wait_ge(c_pe.sem, bpe + (i - 1))
                wsem, wval = zts_written[i]
                sy.wait_ge(wsem, wval)
                cx[s].inc(sy.dma_start(ztl[s][:, :, :],
                                       ZTS[i, :, :, :].rearrange("c p t -> p c t")), 16)

        @blk.tensor
        def _(t):
            t.wait_ge(c_gw.sem, sm_loaded)
            t.wait_ge(c_v.sem, bv)
            for i in range(nt_own):
                s = i % 2
                bank = i % 4
                col = bank * 512
                t.wait_ge(cx[s].sem, bx[s] + (i // 2 + 1) * 16)
                if i >= 4:
                    t.wait_ge(c_v.sem, bv + (i - 4) + 1)
                for dck in range(NDC):
                    t.matmul(ps[:, col:col + SMW], ztl[s][:, dck, :], smw[:, dck, :],
                             start=(dck == 0), stop=False)
                c_pe.inc(t.matmul(ps[:, col + SMW - 1:col + SMW], ones[:, 0:128],
                                  bthr_sb[:, :], start=False, stop=True))

        @blk.vector
        def _(v):
            for i in range(nt_own):
                bank = i % 4
                col = bank * 512
                v.wait_ge(c_pe.sem, bpe + i + 1)
                v.tensor_copy(h_all[:, i, :], ps[:, col:col + 128])
                v.tensor_copy(gate_all[:, i, :], ps[:, col + 128:col + 136])
                c_v.inc(v.tensor_copy(thr_all[:, i:i + 1], ps[:, col + 136:col + 137]))

    p6a_end = snap()

    B = snap()
    bv, ba, btp = B[id(c_v)], B[id(c_a)], B[id(c_tp)]

    if True:
        @blk.vector
        def _(v):
            v.wait_ge(c_v.sem, bv)
            c_v.inc(v.tensor_reduce(gmax[:, :].rearrange("p (c f) -> p c f", f=1),
                                    gate_all[:, :, :], AX, ALU.max))
            v.wait_ge(c_v.sem, bv + 1)
            c_v.inc(v.tensor_tensor(gexp[:, :, :], gate_all[:, :, :],
                                    gmax[:, :].rearrange("p (c f) -> p c f", f=1)
                                    .broadcast_to([128, nt_own, 8]),
                                    ALU.subtract))

        @blk.scalar
        def _(sc):
            sc.wait_ge(c_v.sem, bv + 2)
            c_a.inc(sc.activation(gexp[:, :, :].rearrange("p c f -> p (c f)"),
                                  gexp[:, :, :].rearrange("p c f -> p (c f)"), AF.Exp))
            sc.wait_ge(c_a.sem, ba + 1)
            c_a.inc(sc.activation(sig[:, :], thr_all[:, :], AF.Sigmoid))

        @blk.vector
        def _(v):
            v.wait_ge(c_a.sem, ba + 2)
            c_v.inc(v.tensor_reduce(gsum[:, :].rearrange("p (c f) -> p c f", f=1),
                                    gexp[:, :, :], AX, ALU.add))
            v.wait_ge(c_v.sem, bv + 3)
            c_v.inc(v.tensor_scalar_mul(gsum[:, :], gsum[:, :], -0.125))
            v.wait_ge(c_v.sem, bv + 4)
            c_v.inc(v.tensor_mul(sig[:, :], sig[:, :], gsum[:, :]))
            v.wait_ge(c_v.sem, bv + 5)
            c_v.inc(v.tensor_tensor(wrelu[:, :, :], gexp[:, :, :],
                                    sig[:, :].rearrange("p (c f) -> p c f", f=1)
                                    .broadcast_to([128, nt_own, 8]), ALU.add))
            v.wait_ge(c_v.sem, bv + 6)
            c_v.inc(v.tensor_scalar_max(wrelu[:, :, :], wrelu[:, :, :], 0.0))
            v.wait_ge(c_v.sem, bv + 7)
            c_v.inc(v.tensor_reduce(wsum[:, :].rearrange("p (c f) -> p c f", f=1),
                                    wrelu[:, :, :], AX, ALU.add))
            v.wait_ge(c_v.sem, bv + 8)
            c_v.inc(v.tensor_single_scalar(gmax[:, :], wsum[:, :], 0.0, ALU.is_equal))
            v.wait_ge(c_v.sem, bv + 9)
            c_v.inc(v.tensor_add(wsum[:, :], wsum[:, :], gmax[:, :]))
            v.wait_ge(c_v.sem, bv + 10)
            c_v.inc(v.reciprocal(gsum[:, :], wsum[:, :]))
            v.wait_ge(c_v.sem, bv + 11)
            c_v.inc(v.tensor_scalar_mul(gsum[:, :], gsum[:, :], 2.0))
            v.wait_ge(c_v.sem, bv + 12)
            c_v.inc(v.tensor_tensor(wrelu[:, :, :], wrelu[:, :, :],
                                    gsum[:, :].rearrange("p (c f) -> p c f", f=1)
                                    .broadcast_to([128, nt_own, 8]), ALU.mult))
            v.wait_ge(c_v.sem, bv + 13)
            for e in range(8):
                ins = v.tensor_tensor(hw[:, :, e * 16:(e + 1) * 16],
                                      h_all[:, :, e * 16:(e + 1) * 16],
                                      wrelu[:, :, e:e + 1].broadcast_to([128, nt_own, 16]),
                                      ALU.mult)
            c_v.inc(ins)

        @blk.sync
        def _(sy):
            sy.wait_ge(c_v.sem, bv + 14)
            c_tp.inc(sy.dma_start(hwT[:, :, :], hw[:, :, :], transpose=True), 16)

    hwT_ready = c_tp.val
    al.free("hw", "wsum", "wrelu", "sig", "gsum", "gexp", "gmax",
            "thr_all", "gate_all", "h_all", "smw")
    p6b_end = snap()

    x1l = [al.sbuf(f"x1l{i}", [128, 1024], F32) for i in range(2)]
    outb = [al.sbuf(f"outb{i}", [128, 1024], F32) for i in range(2)]

    B = snap()
    bgw = B[id(c_gw)]
    bx = [B[id(cx[0])], B[id(cx[1])]]
    bpe, bv = B[id(c_pe)], B[id(c_v)]
    bso = [B[id(cso[0])], B[id(cso[1])]]

    blkg.__exit__(None, None, None)
    blkg = nc.Block()
    blk = blkg.__enter__()
    if True:
        @blk.gpsimd
        def _(g):
            for q in range(2, 4):
                for dck in range(NDC):
                    g.wait_ge(c_pe.sem, bpe + (q - 1) * nt_own + min(dck // 2, nt_own - 1))
                    ins = g.dma_start(
                        wb[q % 2][:, dck, :],
                        WBT[dck * 128:(dck + 1) * 128, q * 1024:(q + 1) * 1024])
                    cwb[q % 2].inc(ins, 16)
                wb_loaded[q] = (cwb[q % 2].sem, cwb[q % 2].val)

        @blk.sync
        def _(sy):
            sy.wait_ge(c_pe.sem, p6b_end[id(c_pe)])
            sy.wait_ge(c_v.sem, p6b_end[id(c_v)])
            sy.wait_ge(cx[0].sem, bx[0])
            sy.wait_ge(cx[1].sem, bx[1])

            def store_of(jt):
                js = jt % 2
                sy.wait_ge(c_v.sem, bv + jt + 1)
                jq, ji = jt // nt_own, jt % nt_own
                cso[js].inc(sy.dma_start(
                    OUT[ji * 128:(ji + 1) * 128, jq * 1024:(jq + 1) * 1024],
                    outb[js][:, :]), 16)
            for q in range(4):
                for i in range(nt_own):
                    it = q * nt_own + i
                    s = it % 2
                    if it >= 2:
                        sy.wait_ge(c_v.sem, bv + (it - 1))
                        sy.wait_ge(c_pe.sem, bpe + (it - 1))
                    cx[s].inc(sy.dma_start(
                        x1l[s][:, :],
                        X1S[i * 128:(i + 1) * 128, q * 1024:(q + 1) * 1024]), 16)
                    cx[s].inc(sy.dma_start(ztl[s][:, :, :],
                                           ZTS[i, :, :, :].rearrange("c p t -> p c t")), 16)
                    if it >= 1:
                        store_of(it - 1)
            store_of(4 * nt_own - 1)

        @blk.tensor
        def _(t):
            for q in range(4):
                t.wait_ge(*wb_loaded[q])
                for i in range(nt_own):
                    it = q * nt_own + i
                    s = it % 2
                    pcol = s * 1024
                    t.wait_ge(cx[s].sem, bx[s] + (it // 2 + 1) * 32)
                    if it >= 2:
                        t.wait_ge(c_v.sem, bv + (it - 1))
                    for dck in range(NDC):
                        for ns in range(2):
                            t.matmul(ps[:, pcol + ns * 512:pcol + (ns + 1) * 512],
                                     ztl[s][:, dck, :],
                                     wb[q % 2][:, dck, ns * 512:(ns + 1) * 512],
                                     start=(dck == 0), stop=False)
                    for ns in range(2):
                        t.matmul(ps[:, pcol + ns * 512:pcol + (ns + 1) * 512],
                                 ones[:, 0:128],
                                 bb_sb[:, q * 1024 + ns * 512:q * 1024 + (ns + 1) * 512],
                                 start=False, stop=False)
                    if it == 0:
                        t.wait_ge(c_tp.sem, hwT_ready)
                    for ns in range(2):
                        ins = t.matmul(ps[:, pcol + ns * 512:pcol + (ns + 1) * 512],
                                       hwT[:, i, :],
                                       bwp[:, q * 1024 + ns * 512:q * 1024 + (ns + 1) * 512],
                                       start=False, stop=True)
                    c_pe.inc(ins)

        @blk.vector
        def _(v):
            for q in range(4):
                for i in range(nt_own):
                    it = q * nt_own + i
                    s = it % 2
                    v.wait_ge(c_pe.sem, bpe + it + 1)
                    v.wait_ge(cx[s].sem, bx[s] + (it // 2 + 1) * 32)
                    if it >= 2:
                        v.wait_ge(cso[s].sem, bso[s] + (it // 2) * 16)
                    c_v.inc(v.tensor_tensor(outb[s][:, :], ps[:, s * 1024:s * 1024 + 1024],
                                            x1l[s][:, :], ALU.add))

        @blk.sync
        def _(sy):
            for c in [cso[0], cso[1], cx[0], cx[1], csw[0], csw[1],
                      csz[0], csz[1]]:
                sy.wait_ge(c.sem, c.val)

        @blk.gpsimd
        def _(g):
            g.wait_ge(c_gw.sem, c_gw.val)
            g.wait_ge(cwb[0].sem, cwb[0].val)
            g.wait_ge(cwb[1].sem, cwb[1].val)
            g.wait_ge(cgx[0].sem, cgx[0].val)
            g.wait_ge(cgx[1].sem, cgx[1].val)

        @blk.scalar
        def _(sc):
            sc.wait_ge(c_tp.sem, c_tp.val)
            sc.wait_ge(c_tq.sem, c_tq.val)
            sc.wait_ge(c_ta.sem, c_ta.val)
            sc.wait_ge(cqs[0].sem, cqs[0].val)
            sc.wait_ge(cqs[1].sem, cqs[1].val)
            sc.wait_ge(ctz[0].sem, ctz[0].val)
            sc.wait_ge(ctz[1].sem, ctz[1].val)

    blkg.__exit__(None, None, None)
    al.free_all()
    psg.__exit__(None, None, None)
    for c in ctrs:
        c.guard.__exit__(None, None, None)
    nc.finalize()
    return nc


def kernel(L0_attn_x, L0_attn_Wqkv, L0_attn_bqkv, L0_attn_Wo, L0_attn_bo,
           L0_ffn_W_base, L0_ffn_b_base, L0_ffn_W_gate, L0_ffn_W_thr,
           L0_ffn_b_thr, L0_ffn_A_w, L0_ffn_B_w):
    x = np.asarray(L0_attn_x, dtype=np.float32)
    Wqkv = np.ascontiguousarray(np.asarray(L0_attn_Wqkv, dtype=np.float32))
    bqkv = np.asarray(L0_attn_bqkv, dtype=np.float32).reshape(1, 3 * DH)
    Wo = np.ascontiguousarray(np.asarray(L0_attn_Wo, dtype=np.float32))
    bo = np.asarray(L0_attn_bo, dtype=np.float32).reshape(1, D)
    WBT = np.ascontiguousarray(np.asarray(L0_ffn_W_base, dtype=np.float32).T)
    bb = np.asarray(L0_ffn_b_base, dtype=np.float32).reshape(1, D)
    Wg = np.asarray(L0_ffn_W_gate, dtype=np.float32)
    Wt = np.asarray(L0_ffn_W_thr, dtype=np.float32)
    bt = np.asarray(L0_ffn_b_thr, dtype=np.float32).reshape(1, 1)
    Aw = np.asarray(L0_ffn_A_w, dtype=np.float32)
    Bw = np.asarray(L0_ffn_B_w, dtype=np.float32)

    SM = np.ascontiguousarray(np.concatenate(
        [Aw.reshape(ER, D).T, Wg.T, Wt.reshape(D, 1)], axis=1))
    BWP = np.ascontiguousarray(Bw.transpose(0, 2, 1).reshape(ER, D))

    n_all = x.shape[1]
    n_own = n_all // 2
    nc = build_program(n_all, n_own)

    in_maps = []
    for c in range(8):
        b, h = c // 2, c % 2
        own = x[b, h * n_own:(h + 1) * n_own]
        other = x[b, (1 - h) * n_own:(2 - h) * n_own]
        xb = np.ascontiguousarray(np.concatenate([own, other], axis=0))
        in_maps.append({
            "XB": xb, "WQKV": Wqkv, "BQKV": bqkv, "WO": Wo, "BO": bo,
            "WBT": WBT, "BBASE": bb, "SMALLW": SM, "BTHR": bt, "BW": BWP,
        })

    res = run_bass_kernel_spmd(nc, in_maps, core_ids=list(range(8)))
    out = np.empty((x.shape[0], n_all, D), dtype=np.float32)
    for c in range(8):
        b, h = c // 2, c % 2
        out[b, h * n_own:(h + 1) * n_own] = res.results[c]["OUT"]
    return out
